# version 01
# baseline (speedup 1.0000x reference)
import math
import numpy as np
import ml_dtypes
from contextlib import ExitStack

import concourse.bass as bass
import concourse.mybir as mybir
from concourse.bass_utils import run_bass_kernel_spmd

F32 = mybir.dt.float32
BF16 = mybir.dt.bfloat16
AF = mybir.ActivationFunctionType
ALU = mybir.AluOpType
AX = mybir.AxisListType

D = 1024
NH = 16
DH = 64
DHR = 32
DL = 256
DFF = 4096
RANK = 256
WIN = 256
EPS = 1e-5
SCL_MLA = 1.0 / math.sqrt(DH + DHR)
SCL_SWA = 1.0 / math.sqrt(DH)
GC1 = 0.7978845608028654
GC2 = 0.7978845608028654 * 0.044715
CVALS = [EPS, -0.5, SCL_MLA, SCL_SWA, 80.0, -80.0, 1.0, -2.0, GC1, GC2, 0.5, 2.0]

ENGS = ("sync", "scalar", "tensor", "vector", "gpsimd")


def build(nt):
    S = nt * 128
    nc = bass.Bass("TRN2", target_bir_lowering=False, debug=False)
    DT = {}

    def din(name, shape, dt=F32):
        DT[name] = nc.dram_tensor(name, shape, dt, kind="ExternalInput").ap()
        return DT[name]

    din("x", [S, D])
    for w, sh in [("wdq", [128, 8 * 256]), ("wuq", [128, 2 * 1024]), ("wqr", [128, 2 * 512]),
                  ("wdkv", [128, 8 * 256]), ("wuk", [128, 2 * 1024]), ("wuv", [128, 2 * 1024]),
                  ("wkr", [128, 8 * 32]), ("wo", [128, 8 * 1024]),
                  ("u1", [128, 8 * 256]), ("v1", [128, 2 * 4096]), ("u2", [128, 32 * 256]),
                  ("v2", [128, 2 * 1024]),
                  ("qw", [128, 8 * 1024]), ("kw", [128, 8 * 1024]), ("vw", [128, 8 * 1024]),
                  ("ow", [128, 8 * 1024])]:
        din(w, sh)
    din("b1r", [128, 32])
    din("qbr", [64, 16])
    din("kbr", [64, 16])
    din("b2row", [1, D])
    din("vbrow", [1, D])
    din("obrow", [1, D])
    din("ropec", [S, 512])
    din("ropes", [S, 512])
    din("mlow", [128, 128])
    din("mhigh", [128, 128])
    din("identbf", [128, 128], BF16)
    din("onescol", [1, 128])
    din("ones128", [128, 1])
    din("cst", [128, len(CVALS)])
    out = nc.dram_tensor("out", [S, D], F32, kind="ExternalOutput").ap()
    kcS = nc.dram_tensor("kcS", [nt, 64, 2048], BF16, kind="Internal").ap()
    krS = nc.dram_tensor("krS", [nt, 32, 2048], BF16, kind="Internal").ap()
    qcS = nc.dram_tensor("qcS", [nt, 64, 2048], BF16, kind="Internal").ap()
    qrS = nc.dram_tensor("qrS", [nt, 32, 2048], BF16, kind="Internal").ap()
    vS = nc.dram_tensor("vS", [nt, 128, NH * 64], BF16, kind="Internal").ap()
    x2S = nc.dram_tensor("x2S", [nt, 128, D], F32, kind="Internal").ap()

    with ExitStack() as stk:
        E = stk.enter_context
        sb = {}

        def T(name, shape, dt=F32):
            sb[name] = E(nc.sbuf_tensor(name, shape, dt))
            return sb[name]

        arena = T("arena", [128, 32768], BF16)

        class ASlice:
            def __init__(self, p0, p1, c0, cols, base=None):
                self.base = arena if base is None else base
                self.p0, self.p1, self.c0, self.cols = p0, p1, c0, cols
                self.shape = (p1 - p0, cols)

            def __getitem__(self, idx):
                ps, cs = idx
                a0 = self.p0 if ps.start is None else self.p0 + ps.start
                a1 = self.p1 if ps.stop is None else self.p0 + ps.stop
                b0 = self.c0 if cs.start is None else self.c0 + cs.start
                b1 = self.c0 + self.cols if cs.stop is None else self.c0 + cs.stop
                return self.base[a0:a1, b0:b1]

        wb = {}
        _off = [0]

        def AV(cols, p0=0, p1=128, at=None):
            c0 = _off[0] if at is None else at
            if at is None:
                _off[0] += cols
            return ASlice(p0, p1, c0, cols)

        for w, cols in [("wdq", 8 * 256), ("wuq", 2 * 1024), ("wqr", 2 * 512), ("wdkv", 8 * 256),
                        ("wuk", 2 * 1024), ("wuv", 2 * 1024), ("wkr", 8 * 32)]:
            wb[w] = AV(cols)
        kcA = AV(2048, 0, 64)
        krA = ASlice(64, 96, kcA.c0, 2048)
        qcA = AV(2048, 0, 64)
        qrA = ASlice(64, 96, qcA.c0, 2048)
        vA = AV(NH * 64)
        wb["wo"] = ASlice(0, 128, 0, 8 * 1024)
        kbuf = [ASlice(0, 96, 8192, 2048), ASlice(0, 96, 10240, 2048)]
        qbuf = ASlice(0, 96, 12288, 2048)
        vbuf = [ASlice(0, 128, 14336, NH * 64), ASlice(0, 128, 14336 + NH * 64, NH * 64)]
        for i, w in enumerate(("qw", "kw", "vw", "ow")):
            wb[w] = ASlice(0, 128, i * 8192, 8 * 1024)
        for w, cols in [("u1", 8 * 256), ("v1", 2 * 4096), ("u2", 32 * 256), ("v2", 2 * 1024)]:
            wb[w] = T("W_" + w, [128, cols], BF16)
        b1r = T("sb_b1r", [128, 32])
        qbr = T("sb_qbr", [64, 16])
        kbr = T("sb_kbr", [64, 16])
        browz = T("browz", [1, D])
        b2bf = T("b2bf", [1, D], BF16)
        vbbf = T("vbbf", [1, D], BF16)
        obbf = T("obbf", [1, D], BF16)
        onesbf = T("onesbf", [1, 128], BF16)
        onecbf = T("onecbf", [128, 1], BF16)
        mlow = T("sb_mlow", [128, 128])
        mhigh = T("sb_mhigh", [128, 128])
        identbf = T("sb_identbf", [128, 128], BF16)
        onescol = T("sb_onescol", [1, 128])
        cstb = T("cstb", [128, len(CVALS)])
        for i, v in enumerate(CVALS):
            nc.const_aps.register(F32, v, cstb[:, i:i + 1])
        stg = [T("stg0", [128, 2048])]

        xa = [T("xa0", [128, D]), T("xa1", [128, D])]
        st12 = T("st12", [128, 12])
        mv = T("mv", [128, 2])
        lnv = T("lnv", [128, 1])
        rstd = T("rstd", [128, 1])
        xnbf = T("xnbf", [128, D], BF16)
        xnT = T("xnT", [128, 8 * 128], BF16)
        rc = [T("rc0", [128, 512])]
        rs = [T("rs0", [128, 512])]
        qrbf = T("qrbf", [128, 512], BF16)
        krbf = T("krbf", [128, 32], BF16)
        krT = T("krT", [32, 128], BF16)
        ckv = T("ckv", [128, 2 * 128], BF16)
        cq = T("cq", [128, 2 * 128], BF16)
        ptb = [T("ptb0", [128, 2048], BF16), T("ptb1", [128, 2048], BF16)]
        attn_sb = T("attn_sb", [128, NH * 64], BF16)
        rec_sb = T("rec_sb", [128, NH])
        attnT = T("attnT", [128, 8 * 128], BF16)
        x1sb = T("x1sb", [128, D])
        x2sb = [T("x2sb0", [128, D]), T("x2sb1", [128, D])]
        ptf = T("ptf", [128, 256], BF16)
        hb = T("hb", [128, 512])
        bufA = T("bufA", [128, 512])
        bufB = T("bufB", [128, 512])
        sq, u0, uu, ee, dd, sg = bufA, bufB, bufA, bufB, bufB, bufA
        ropeA, ropeB = bufA, bufB
        htb = T("htb", [128, 32 * 128], BF16)
        q2t = T("q2t", [128, 256], BF16)

        qTs = T("qTs", [64, NH * 128], BF16)
        kring = T("kring", [64, 3 * NH * 128], BF16)
        vring = T("vring", [128, 3 * NH * 64], BF16)
        x3sb = x1sb
        x4sb = x2sb
        x2in = xa

        pbig = E(nc.psum_tensor("pbig", [128, 2048], F32))
        acc = E(nc.psum_tensor("acc", [128, NH * 64], F32))
        pbf = E(nc.psum_tensor("pbf", [128, 512], BF16))
        pf32 = E(nc.psum_tensor("pf32", [128, 336], F32))
        ptrb = ASlice(0, 64, 0, 256, base=pbf)
        ptrb2 = ASlice(0, 128, 256, 256, base=pbf)
        pq64 = ASlice(0, 64, 0, 256, base=pf32)
        pmisc = ASlice(0, 128, 256, 64, base=pf32)
        pden = ASlice(0, 128, 320, 16, base=pf32)

        sems = {}
        for sname in ("wld", "cv", "xal", "rpl", "t", "s", "v", "gw", "kld0", "kld1", "qld",
                      "xld", "x2st", "x2ld", "ost", "ph"):
            sems[sname] = E(nc.semaphore(sname))

        DT_ones128_f32 = T("ones128f", [128, 1])
        DT_ones128 = DT_ones128_f32[:]

        def program(active):
            eng = getattr(nc, active)
            cnt = {k: 0 for k in sems}

            def emit(engine, f, incs=()):
                r = f(getattr(nc, engine)) if engine == active else None
                for sname, amt in incs:
                    cnt[sname] += amt
                    if r is not None:
                        r.then_inc(sems[sname], amt)
                return cnt

            def wait(engine, sname, val=None):
                v = cnt[sname] if val is None else val
                if v <= 0:
                    return
                if engine == active:
                    eng.wait_ge(sems[sname], v)

            def barrier():
                for sn in ("wld", "xal", "rpl", "kld0", "kld1", "qld", "x2ld"):
                    wait("sync", sn)
                for sn in ("gw", "x2st", "ost"):
                    wait("gpsimd", sn)
                for e in ENGS:
                    if e == active:
                        eng.sem_inc(sems["ph"], 1)
                    cnt["ph"] += 1
                tgt = cnt["ph"]
                for e in ENGS:
                    wait(e, "ph", tgt)

            smalls = [("mlow", mlow), ("mhigh", mhigh), ("identbf", identbf),
                      ("onescol", onescol), ("ones128", DT_ones128_f32), ("cst", cstb),
                      ("b1r", b1r), ("qbr", qbr), ("kbr", kbr)]
            for name, dst in smalls:
                emit("sync", lambda e, n=name, d=dst: e.dma_start(out=d[:], in_=DT[n][:]),
                     incs=[("wld", 16)])
            wait("scalar", "wld")
            emit("scalar", lambda e: e.copy(out=onesbf[:], in_=onescol[:]), incs=[("cv", 1)])
            emit("scalar", lambda e: e.copy(out=onecbf[:], in_=DT_ones128_f32[:]), incs=[("cv", 1)])
            for name, dst in (("b2row", b2bf), ("vbrow", vbbf), ("obrow", obbf)):
                wait("sync", "cv")
                emit("sync", lambda e, n=name: e.dma_start(out=browz[:], in_=DT[n][:]), incs=[("wld", 16)])
                wait("scalar", "wld")
                emit("scalar", lambda e, d=dst: e.copy(out=d[:], in_=browz[:]), incs=[("cv", 1)])
            wconv = [(w, wb[w].shape[1]) for w in ("wdq", "wuq", "wqr", "wdkv", "wuk", "wuv",
                                                   "wkr", "u1", "v1", "u2", "v2")]
            wconv_swa = [(w, 8192) for w in ("qw", "kw", "vw", "ow")]

            def conv_weights(lst):
                si = [0]
                for w, cols in lst:
                    for c0 in range(0, cols, 2048):
                        cw = min(2048, cols - c0)
                        wait("sync", "cv")
                        emit("sync", lambda e, w=w, c0=c0, cw=cw: e.dma_start(
                            out=stg[0][:, 0:cw], in_=DT[w][:, c0:c0 + cw]), incs=[("wld", 16)])
                        wait("scalar", "wld")
                        emit("scalar", lambda e, w=w, c0=c0, cw=cw: e.copy(
                            out=wb[w][:, c0:c0 + cw], in_=stg[0][:, 0:cw]), incs=[("cv", 1)])
                        si[0] += 1

            conv_weights(wconv)
            barrier()

            def layernorm(xin, war_t=None):
                wait("vector", "t", war_t)
                wait("vector", "v")
                emit("vector", lambda e: e.bn_stats(out=st12[:, 0:6], in_=xin[:, 0:512]), incs=[("v", 1)])
                emit("vector", lambda e: e.bn_stats(out=st12[:, 6:12], in_=xin[:, 512:1024]), incs=[("v", 1)])
                wait("vector", "v")
                emit("vector", lambda e: e.bn_aggr(out=mv[:], in_=st12[:]), incs=[("v", 1)])
                wait("scalar", "v")
                emit("scalar", lambda e: e.activation(out=lnv[:], in_=mv[:, 1:2], func=AF.Ln, bias=EPS),
                     incs=[("s", 1)])
                wait("scalar", "s")
                emit("scalar", lambda e: e.activation(out=rstd[:], in_=lnv[:], func=AF.Exp, scale=-0.5),
                     incs=[("s", 1)])
                wait("vector", "s")
                emit("vector", lambda e: e.tensor_scalar(out=xnbf[:], in0=xin[:], scalar1=mv[:, 0:1],
                                                         scalar2=rstd[:], op0=ALU.subtract, op1=ALU.mult),
                     incs=[("v", 1)])

            def transposes_xn():
                for c in range(8):
                    b = c % 2
                    wait("tensor", "v")
                    wait("tensor", "s")
                    emit("tensor", lambda e, c=c, b=b: e.transpose(
                        ptrb2[:, b * 128:(b + 1) * 128], xnbf[:, c * 128:(c + 1) * 128], identbf[:]),
                        incs=[("t", 1)])
                    wait("scalar", "t")
                    emit("scalar", lambda e, c=c, b=b: e.copy(
                        out=xnT[:, c * 128:(c + 1) * 128], in_=ptrb2[:, b * 128:(b + 1) * 128]),
                        incs=[("s", 1)])

            def ffn(xin, xout):
                layernorm(xin)
                transposes_xn()
                wait("tensor", "s")
                for l in range(2):
                    for c in range(8):
                        emit("tensor", lambda e, l=l, c=c: e.matmul(
                            pbig[:, l * 128:(l + 1) * 128],
                            wb["u1"][:, c * 256 + l * 128:c * 256 + (l + 1) * 128],
                            xnT[:, c * 128:(c + 1) * 128],
                            start=(l == 0 and c == 0), stop=(l == 1 and c == 7)),
                            incs=([("t", 1)] if c == 7 else []))
                wait("scalar", "t")
                emit("scalar", lambda e: e.copy(out=ptf[:], in_=pbig[:, 0:256]), incs=[("s", 1)])
                for g in range(8):
                    pb = 1024 + (g % 2) * 512
                    wait("tensor", "s")
                    for jj in range(4):
                        j = g * 4 + jj
                        for l in range(2):
                            emit("tensor", lambda e, j=j, l=l, pb=pb, jj=jj: e.matmul(
                                pbig[:, pb + jj * 128:pb + (jj + 1) * 128],
                                wb["v1"][:, l * 4096 + j * 128:l * 4096 + (j + 1) * 128],
                                ptf[:, l * 128:(l + 1) * 128],
                                start=(jj == 0 and l == 0), stop=(jj == 3 and l == 1)),
                                incs=([("t", 1)] if (jj == 3 and l == 1) else []))
                    wait("scalar", "t")
                    wait("scalar", "v")
                    for jj in range(4):
                        j = g * 4 + jj
                        emit("scalar", lambda e, j=j, jj=jj, pb=pb: e.activation(
                            out=hb[:, jj * 128:(jj + 1) * 128], in_=pbig[:, pb + jj * 128:pb + (jj + 1) * 128],
                            func=AF.Identity, bias=b1r[:, j:j + 1]), incs=[("s", 1)])
                    wait("scalar", "s")
                    emit("scalar", lambda e: e.activation(out=sq[:], in_=hb[:], func=AF.Square),
                         incs=[("s", 1)])
                    wait("vector", "s")
                    emit("vector", lambda e: e.tensor_scalar(out=u0[:], in0=sq[:], scalar1=GC2,
                                                             scalar2=GC1, op0=ALU.mult, op1=ALU.add),
                         incs=[("v", 1)])
                    wait("vector", "v")
                    emit("vector", lambda e: e.tensor_tensor(out=uu[:], in0=u0[:], in1=hb[:], op=ALU.mult),
                         incs=[("v", 1)])
                    wait("scalar", "v")
                    emit("scalar", lambda e: e.activation(out=ee[:], in_=uu[:], func=AF.Exp, scale=-2.0),
                         incs=[("s", 1)])
                    wait("vector", "s")
                    emit("vector", lambda e: e.tensor_scalar_add(out=dd[:], in0=ee[:], scalar1=1.0),
                         incs=[("v", 1)])
                    wait("vector", "v")
                    emit("vector", lambda e: e.reciprocal(out=sg[:], in_=dd[:]), incs=[("v", 1)])
                    wait("vector", "v")
                    emit("vector", lambda e, g=g: e.tensor_tensor(out=htb[:, g * 512:(g + 1) * 512],
                                                                  in0=hb[:], in1=sg[:], op=ALU.mult),
                         incs=[("v", 1)])
                wait("tensor", "v")
                for l in range(2):
                    for j in range(32):
                        emit("tensor", lambda e, l=l, j=j: e.matmul(
                            pbig[:, 256 + l * 128:256 + (l + 1) * 128],
                            wb["u2"][:, j * 256 + l * 128:j * 256 + (l + 1) * 128],
                            htb[:, j * 128:(j + 1) * 128],
                            start=(l == 0 and j == 0), stop=(l == 1 and j == 31)),
                            incs=([("t", 1)] if j == 31 else []))
                wait("scalar", "t")
                emit("scalar", lambda e: e.copy(out=q2t[:], in_=pbig[:, 256:512]), incs=[("s", 1)])
                wait("tensor", "s")
                for hf in range(2):
                    for l in range(2):
                        emit("tensor", lambda e, l=l, hf=hf: e.matmul(
                            pbig[:, 1024 + hf * 512:1536 + hf * 512], q2t[:, l * 128:(l + 1) * 128],
                            wb["v2"][:, l * 1024 + hf * 512:l * 1024 + hf * 512 + 512],
                            start=(l == 0), stop=False))
                    emit("tensor", lambda e, hf=hf: e.matmul(
                        pbig[:, 1024 + hf * 512:1536 + hf * 512], onesbf[:], b2bf[:, hf * 512:hf * 512 + 512],
                        start=False, stop=True), incs=([("t", 1)] if hf == 1 else []))
                wait("vector", "t")
                emit("vector", lambda e, xin=xin, xout=xout: e.tensor_tensor(
                    out=xout[:], in0=pbig[:, 1024:2048], in1=xin[:], op=ALU.add), incs=[("v", 1)])

            xv = DT["x"].rearrange("(n p) d -> n p d", p=128)
            rcv = DT["ropec"].rearrange("(n p) d -> n p d", p=128)
            rsv = DT["ropes"].rearrange("(n p) d -> n p d", p=128)
            for t in range(nt):
                b = t % 2
                if t >= 1:
                    wait("sync", "v")
                emit("sync", lambda e, t=t, b=b: e.dma_start(out=xa[b][:], in_=xv[t]), incs=[("xal", 16)])
                emit("sync", lambda e, t=t: e.dma_start(out=rc[0][:], in_=rcv[t]), incs=[("rpl", 16)])
                emit("sync", lambda e, t=t: e.dma_start(out=rs[0][:], in_=rsv[t]), incs=[("rpl", 16)])
                wait("vector", "xal")
                layernorm(xa[b])
                transposes_xn()
                wait("tensor", "s")
                for i, (wname, dst) in enumerate((("wdkv", ckv), ("wdq", cq))):
                    for l in range(2):
                        for c in range(8):
                            emit("tensor", lambda e, i=i, l=l, c=c, wname=wname: e.matmul(
                                pbig[:, i * 256 + l * 128:i * 256 + (l + 1) * 128],
                                wb[wname][:, c * 256 + l * 128:c * 256 + (l + 1) * 128],
                                xnT[:, c * 128:(c + 1) * 128],
                                start=(i == 0 and l == 0 and c == 0),
                                stop=(i == 1 and l == 1 and c == 7)),
                                incs=([("t", 1)] if c == 7 else []))
                wait("scalar", "t")
                for i, dst in ((0, ckv), (1, cq)):
                    emit("scalar", lambda e, i=i, dst=dst: e.copy(out=dst[:], in_=pbig[:, i * 256:i * 256 + 256]),
                         incs=[("s", 1)])
                if t >= 1:
                    wait("scalar", "gw", t * 80)
                    wait("vector", "gw", t * 80)
                for i, (wname, csrc, dstA) in enumerate((("wuk", ckv, kcA), ("wuq", cq, qcA))):
                    for p in range(8):
                        pb = 512 + i * 256 + (p % 2) * 128
                        wait("tensor", "s")
                        for l in range(2):
                            emit("tensor", lambda e, p=p, l=l, pb=pb, wname=wname, csrc=csrc: e.matmul(
                                pbig[:, pb:pb + 128],
                                wb[wname][:, l * 1024 + p * 128:l * 1024 + (p + 1) * 128],
                                csrc[:, l * 128:(l + 1) * 128],
                                start=(l == 0), stop=(l == 1)), incs=([("t", 1)] if l == 1 else []))
                        wait("scalar", "t")
                        emit("scalar", lambda e, p=p, pb=pb, dstA=dstA: e.copy(
                            out=dstA[0:64, (2 * p) * 128:(2 * p) * 128 + 128], in_=pbig[0:64, pb:pb + 128]),
                            incs=[("s", 1)])
                        emit("scalar", lambda e, p=p, pb=pb, dstA=dstA: e.copy(
                            out=dstA[0:64, (2 * p + 1) * 128:(2 * p + 1) * 128 + 128],
                            in_=pbig[64:128, pb:pb + 128]), incs=[("s", 1)])
                wait("tensor", "s")
                for half in range(2):
                    for l in range(2):
                        emit("tensor", lambda e, l=l, hf=half: e.matmul(
                            pbig[:, 1024 + hf * 512:1536 + hf * 512], ckv[:, l * 128:(l + 1) * 128],
                            wb["wuv"][:, l * 1024 + hf * 512:l * 1024 + hf * 512 + 512],
                            start=(l == 0), stop=(l == 1)),
                            incs=([("t", 1)] if (l == 1 and half == 1) else []))
                wait("scalar", "t")
                emit("scalar", lambda e: e.copy(out=vA[:, 0:NH * 64], in_=pbig[:, 1024:2048]),
                     incs=[("s", 1)])
                wait("tensor", "s")
                for l in range(2):
                    emit("tensor", lambda e, l=l: e.matmul(
                        pbig[:, 0:512], cq[:, l * 128:(l + 1) * 128],
                        wb["wqr"][:, l * 512:(l + 1) * 512],
                        start=(l == 0), stop=(l == 1)), incs=([("t", 1)] if l == 1 else []))
                for c in range(8):
                    emit("tensor", lambda e, c=c: e.matmul(
                        pmisc[:, 0:32], xnT[:, c * 128:(c + 1) * 128],
                        wb["wkr"][:, c * 32:(c + 1) * 32],
                        start=(c == 0), stop=(c == 7)), incs=([("t", 1)] if c == 7 else []))
                wait("vector", "t")
                wait("vector", "rpl")
                emit("vector", lambda e: e.tensor_tensor(out=ropeA[:], in0=pbig[:, 0:512], in1=rc[0][:],
                                                              op=ALU.mult), incs=[("v", 1)])
                emit("vector", lambda e: e.tensor_tensor(out=ropeB[:], in0=pbig[:, 0:512], in1=rs[0][:],
                                                              op=ALU.mult), incs=[("v", 1)])
                wait("vector", "v")
                r3 = lambda ap: ap.rearrange("p (h two f) -> p h two f", two=2, f=16)
                emit("vector", lambda e: e.tensor_tensor(
                    out=r3(qrbf[:])[:, :, 0, :], in0=r3(ropeA[:])[:, :, 0, :],
                    in1=r3(ropeB[:])[:, :, 1, :], op=ALU.subtract), incs=[("v", 1)])
                emit("vector", lambda e: e.tensor_tensor(
                    out=r3(qrbf[:])[:, :, 1, :], in0=r3(ropeB[:])[:, :, 0, :],
                    in1=r3(ropeA[:])[:, :, 1, :], op=ALU.add), incs=[("v", 1)])
                wait("vector", "v")
                emit("vector", lambda e: e.tensor_tensor(out=ropeA[:, 0:32], in0=pmisc[:, 0:32],
                                                              in1=rc[0][:, 0:32], op=ALU.mult), incs=[("v", 1)])
                emit("vector", lambda e: e.tensor_tensor(out=ropeB[:, 0:32], in0=pmisc[:, 0:32],
                                                              in1=rs[0][:, 0:32], op=ALU.mult), incs=[("v", 1)])
                wait("vector", "v")
                emit("vector", lambda e: e.tensor_tensor(out=krbf[:, 0:16], in0=ropeA[:, 0:16],
                                                         in1=ropeB[:, 16:32], op=ALU.subtract), incs=[("v", 1)])
                emit("vector", lambda e: e.tensor_tensor(out=krbf[:, 16:32], in0=ropeB[:, 0:16],
                                                         in1=ropeA[:, 16:32], op=ALU.add), incs=[("v", 1)])
                wait("tensor", "v")
                for h in range(NH):
                    pb2 = (h % 2) * 128
                    wait("tensor", "s")
                    emit("tensor", lambda e, h=h, pb2=pb2: e.transpose(
                        ptrb[0:32, pb2:pb2 + 128], qrbf[:, h * 32:(h + 1) * 32], identbf[:]),
                        incs=[("t", 1)])
                    wait("scalar", "t")
                    emit("scalar", lambda e, h=h, pb2=pb2: e.copy(
                        out=qrA[0:32, h * 128:(h + 1) * 128], in_=ptrb[0:32, pb2:pb2 + 128]),
                        incs=[("s", 1)])
                wait("tensor", "s")
                emit("tensor", lambda e: e.transpose(ptrb[0:32, 0:128], krbf[:], identbf[:]), incs=[("t", 1)])
                wait("scalar", "t")
                emit("scalar", lambda e: e.copy(out=krT[:], in_=ptrb[0:32, 0:128]), incs=[("s", 1)])
                wait("vector", "s")
                for h in range(NH):
                    emit("vector", lambda e, h=h: e.tensor_copy(out=krA[0:32, h * 128:(h + 1) * 128],
                                                                in_=krT[:]),
                         incs=([("v", 1)] if h == NH - 1 else []))
                wait("gpsimd", "s")
                wait("gpsimd", "v")
                emit("gpsimd", lambda e, t=t: e.dma_start(out=kcS[t], in_=kcA[0:64, :]), incs=[("gw", 16)])
                emit("gpsimd", lambda e, t=t: e.dma_start(out=krS[t], in_=krA[0:32, :]), incs=[("gw", 16)])
                emit("gpsimd", lambda e, t=t: e.dma_start(out=qcS[t], in_=qcA[0:64, :]), incs=[("gw", 16)])
                emit("gpsimd", lambda e, t=t: e.dma_start(out=qrS[t], in_=qrA[0:32, :]), incs=[("gw", 16)])
                emit("gpsimd", lambda e, t=t: e.dma_start(out=vS[t], in_=vA[:, 0:NH * 64]), incs=[("gw", 16)])
            barrier()

            conv_weights([("wo", 8192)])
            g = [0]
            pvmark = {}
            scmark = [0]
            finmark = [0]
            kvld = {}
            for t in range(nt):
                b = t % 2
                if t >= 1:
                    wait("sync", "v")
                emit("sync", lambda e, t=t: e.dma_start(out=qbuf[0:64, :], in_=qcS[t]), incs=[("qld", 16)])
                emit("sync", lambda e, t=t: e.dma_start(out=qbuf[64:96, :], in_=qrS[t]), incs=[("qld", 16)])
                emit("sync", lambda e, t=t, b=b: e.dma_start(out=xa[b][:], in_=xv[t]), incs=[("xal", 16)])
                kb0 = g[0] % 2
                sld = "kld%d" % kb0
                wait("sync", "t", pvmark.get(("k", kb0), 0))
                emit("sync", lambda e, kb0=kb0: e.dma_start(out=kbuf[kb0][0:64, :], in_=kcS[0]),
                     incs=[(sld, 16)])
                emit("sync", lambda e, kb0=kb0: e.dma_start(out=kbuf[kb0][64:96, :], in_=krS[0]),
                     incs=[(sld, 16)])
                emit("sync", lambda e, kb0=kb0: e.dma_start(out=vbuf[kb0][:, :], in_=vS[0]),
                     incs=[(sld, 16)])
                kvld[(t, 0)] = (sld, cnt[sld])
                for k in range(t + 1):
                    gk = g[0] + k
                    kb = gk % 2
                    if k + 1 <= t:
                        nb = (gk + 1) % 2
                        snb = "kld%d" % nb
                        wait("sync", "t", pvmark.get(("k", nb), 0))
                        emit("sync", lambda e, k=k, nb=nb: e.dma_start(out=kbuf[nb][0:64, :], in_=kcS[k + 1]),
                             incs=[(snb, 16)])
                        emit("sync", lambda e, k=k, nb=nb: e.dma_start(out=kbuf[nb][64:96, :], in_=krS[k + 1]),
                             incs=[(snb, 16)])
                        emit("sync", lambda e, k=k, nb=nb: e.dma_start(out=vbuf[nb][:, :], in_=vS[k + 1]),
                             incs=[(snb, 16)])
                        kvld[(t, k + 1)] = (snb, cnt[snb])
                    wait("tensor", kvld[(t, k)][0], kvld[(t, k)][1])
                    wait("tensor", "qld")
                    if k == 0:
                        wait("tensor", "v")
                        wait("tensor", "s", scmark[0])
                    for h in range(NH):
                        emit("tensor", lambda e, h=h, kb=kb: e.matmul(
                            pbig[:, h * 128:(h + 1) * 128],
                            kbuf[kb][0:96, h * 128:(h + 1) * 128],
                            qbuf[0:96, h * 128:(h + 1) * 128],
                            start=(h % 4 == 0), stop=(h % 4 == 3)),
                            incs=([("t", 1)] if h == NH - 1 else []))
                    wait("vector", "t")
                    wait("vector", "s", scmark[0])
                    emit("vector", lambda e: e.tensor_scalar(out=pbig[:, 0:2048], in0=pbig[:, 0:2048],
                                                             scalar1=80.0, scalar2=-80.0,
                                                             op0=ALU.min, op1=ALU.max), incs=[("v", 1)])
                    wait("scalar", "v")
                    wait("scalar", "t", pvmark.get(("pt", kb), 0))
                    emit("scalar", lambda e, kb=kb: e.activation(out=ptb[kb][:], in_=pbig[:, 0:2048],
                                                                 func=AF.Exp, scale=SCL_MLA),
                         incs=[("s", 1)])
                    scmark[0] = cnt["s"]
                    if k == t:
                        wait("vector", "s")
                        for h in range(NH):
                            emit("vector", lambda e, h=h, kb=kb: e.tensor_tensor(
                                out=ptb[kb][:, h * 128:(h + 1) * 128], in0=ptb[kb][:, h * 128:(h + 1) * 128],
                                in1=mlow[:], op=ALU.mult), incs=([("v", 1)] if h == NH - 1 else []))
                        wait("tensor", "v")
                    else:
                        wait("tensor", "s")
                    for h in range(NH):
                        emit("tensor", lambda e, h=h, kb=kb, k=k, t=t: e.matmul(
                            acc[:, h * 64:(h + 1) * 64], ptb[kb][:, h * 128:(h + 1) * 128],
                            vbuf[kb][:, h * 64:h * 64 + 64],
                            start=(k == 0 and h % 8 == 0), stop=(k == t and h % 8 == 7)))
                    for h in range(NH):
                        emit("tensor", lambda e, h=h, kb=kb, k=k, t=t: e.matmul(
                            pden[:, h:h + 1], ptb[kb][:, h * 128:(h + 1) * 128],
                            onecbf[:],
                            start=(k == 0 and h == 0), stop=(k == t and h == NH - 1)),
                            incs=([("t", 1)] if h == NH - 1 else []))
                    pvmark[("pt", kb)] = cnt["t"]
                    pvmark[("k", kb)] = cnt["t"]
                g[0] += t + 1
                wait("vector", "t")
                emit("vector", lambda e: e.reciprocal(out=rec_sb[:, 0:NH], in_=pden[:, 0:NH]),
                     incs=[("v", 1)])
                wait("vector", "v")
                for h in range(NH):
                    emit("vector", lambda e, h=h: e.tensor_scalar(
                        out=attn_sb[:, h * 64:(h + 1) * 64], in0=acc[:, h * 64:h * 64 + 64],
                        scalar1=rec_sb[:, h:h + 1], scalar2=None, op0=ALU.mult),
                        incs=([("v", 1)] if h == NH - 1 else []))
                finmark[0] = cnt["v"]
                wait("tensor", "v")
                for h in range(NH):
                    pb2 = (h % 2) * 128
                    wait("tensor", "s")
                    emit("tensor", lambda e, h=h, pb2=pb2: e.transpose(
                        ptrb[0:64, pb2:pb2 + 128], attn_sb[:, h * 64:(h + 1) * 64], identbf[:]),
                        incs=[("t", 1)])
                    wait("scalar", "t")
                    emit("scalar", lambda e, h=h, pb2=pb2: e.copy(
                        out=attnT[(h % 2) * 64:(h % 2) * 64 + 64, (h // 2) * 128:(h // 2) * 128 + 128],
                        in_=ptrb[0:64, pb2:pb2 + 128]), incs=[("s", 1)])
                wait("tensor", "s")
                for hf in range(2):
                    for c in range(8):
                        emit("tensor", lambda e, c=c, hf=hf: e.matmul(
                            pbig[:, hf * 512:hf * 512 + 512], attnT[:, c * 128:(c + 1) * 128],
                            wb["wo"][:, c * 1024 + hf * 512:c * 1024 + hf * 512 + 512],
                            start=(c == 0), stop=(c == 7)),
                            incs=([("t", 1)] if (c == 7 and hf == 1) else []))
                wait("vector", "t")
                wait("vector", "xal")
                emit("vector", lambda e, b=b: e.tensor_tensor(out=x1sb[:], in0=pbig[:, 0:1024],
                                                              in1=xa[b][:], op=ALU.add), incs=[("v", 1)])
                if t >= 1:
                    wait("vector", "x2st")
                ffn(x1sb, x2sb[b])
                wait("gpsimd", "v")
                emit("gpsimd", lambda e, t=t, b=b: e.dma_start(out=x2S[t], in_=x2sb[b][:]), incs=[("x2st", 16)])
            barrier()

            conv_weights(wconv_swa)
            swa_sc = [0]
            swa_fin = [0]
            ring_rd = {}
            for t in range(nt):
                b = t % 2
                slot = t % 3
                if t >= 1:
                    wait("sync", "v")
                emit("sync", lambda e, t=t, b=b: e.dma_start(out=x2in[b][:], in_=x2S[t]), incs=[("x2ld", 16)])
                wait("vector", "x2ld")
                layernorm(x2in[b])
                transposes_xn()
                for iw, (wname, dst, bias) in enumerate((("qw", qTs, qbr), ("kw", None, kbr))):
                    for h in range(NH):
                        pb2 = (h % 2) * 128
                        wait("tensor", "s")
                        for c in range(8):
                            emit("tensor", lambda e, h=h, c=c, pb2=pb2, wname=wname: e.matmul(
                                pq64[0:64, pb2:pb2 + 128],
                                wb[wname][:, c * 1024 + h * 64:c * 1024 + h * 64 + 64],
                                xnT[:, c * 128:(c + 1) * 128],
                                start=(c == 0), stop=(c == 7)), incs=([("t", 1)] if c == 7 else []))
                        wait("scalar", "t")
                        if iw == 0:
                            emit("scalar", lambda e, h=h, pb2=pb2: e.activation(
                                out=qTs[0:64, h * 128:(h + 1) * 128], in_=pq64[0:64, pb2:pb2 + 128],
                                func=AF.Identity, bias=qbr[0:64, h:h + 1]), incs=[("s", 1)])
                        else:
                            emit("scalar", lambda e, h=h, pb2=pb2, slot=slot: e.activation(
                                out=kring[0:64, slot * 2048 + h * 128:slot * 2048 + (h + 1) * 128],
                                in_=pq64[0:64, pb2:pb2 + 128],
                                func=AF.Identity, bias=kbr[0:64, h:h + 1]), incs=[("s", 1)])
                wait("tensor", "s")
                for hf in range(2):
                    for c in range(8):
                        emit("tensor", lambda e, c=c, hf=hf: e.matmul(
                            pbig[:, hf * 512:hf * 512 + 512], xnT[:, c * 128:(c + 1) * 128],
                            wb["vw"][:, c * 1024 + hf * 512:c * 1024 + hf * 512 + 512],
                            start=(c == 0), stop=False))
                    emit("tensor", lambda e, hf=hf: e.matmul(
                        pbig[:, hf * 512:hf * 512 + 512], onesbf[:], vbbf[:, hf * 512:hf * 512 + 512],
                        start=False, stop=True), incs=([("t", 1)] if hf == 1 else []))
                wait("vector", "t")
                emit("vector", lambda e, slot=slot: e.tensor_copy(
                    out=vring[:, slot * 1024:(slot + 1) * 1024], in_=pbig[:, 0:1024]),
                    incs=[("v", 1)])
                slots = list(range(max(0, t - 2), t + 1))
                for si, k in enumerate(slots):
                    ks = k % 3
                    wait("tensor", "v")
                    wait("tensor", "s")
                    if si == 0:
                        wait("tensor", "v", swa_fin[0])
                        wait("tensor", "s", swa_sc[0])
                    for h in range(NH):
                        emit("tensor", lambda e, h=h, ks=ks: e.matmul(
                            pbig[:, h * 128:(h + 1) * 128],
                            kring[0:64, ks * 2048 + h * 128:ks * 2048 + (h + 1) * 128],
                            qTs[0:64, h * 128:(h + 1) * 128],
                            start=(h % 4 == 0), stop=(h % 4 == 3)), incs=([("t", 1)] if h == NH - 1 else []))
                    pt2 = ptb[si % 2]
                    wait("scalar", "t")
                    wait("scalar", "v", swa_fin[0])
                    wait("scalar", "t", pvmark.get(("spt", si % 2), 0))
                    emit("scalar", lambda e, pt2=pt2: e.activation(out=pt2[:], in_=pbig[:, 0:2048],
                                                                   func=AF.Exp, scale=SCL_SWA),
                         incs=[("s", 1)])
                    swa_sc[0] = cnt["s"]
                    msk = mlow if k == t else (mhigh if k == t - 2 else None)
                    if msk is not None:
                        wait("vector", "s")
                        for h in range(NH):
                            emit("vector", lambda e, h=h, pt2=pt2, msk=msk: e.tensor_tensor(
                                out=pt2[:, h * 128:(h + 1) * 128], in0=pt2[:, h * 128:(h + 1) * 128],
                                in1=msk[:], op=ALU.mult), incs=([("v", 1)] if h == NH - 1 else []))
                        wait("tensor", "v")
                    else:
                        wait("tensor", "s")
                    for h in range(NH):
                        emit("tensor", lambda e, h=h, pt2=pt2, ks=ks, si=si, n=len(slots): e.matmul(
                            acc[:, h * 64:(h + 1) * 64], pt2[:, h * 128:(h + 1) * 128],
                            vring[:, (ks * NH + h) * 64:(ks * NH + h) * 64 + 64],
                            start=(si == 0 and h % 8 == 0), stop=(si == n - 1 and h % 8 == 7)))
                    for h in range(NH):
                        emit("tensor", lambda e, h=h, pt2=pt2, si=si, n=len(slots): e.matmul(
                            pden[:, h:h + 1], pt2[:, h * 128:(h + 1) * 128],
                            onecbf[:],
                            start=(si == 0 and h == 0), stop=(si == n - 1 and h == NH - 1)),
                            incs=([("t", 1)] if h == NH - 1 else []))
                    pvmark[("spt", si % 2)] = cnt["t"]
                ring_rd[slot] = cnt["t"]
                wait("vector", "t")
                emit("vector", lambda e: e.reciprocal(out=rec_sb[:, 0:NH], in_=pden[:, 0:NH]),
                     incs=[("v", 1)])
                wait("vector", "v")
                for h in range(NH):
                    emit("vector", lambda e, h=h: e.tensor_scalar(
                        out=attn_sb[:, h * 64:(h + 1) * 64], in0=acc[:, h * 64:h * 64 + 64],
                        scalar1=rec_sb[:, h:h + 1], scalar2=None, op0=ALU.mult),
                        incs=([("v", 1)] if h == NH - 1 else []))
                swa_fin[0] = cnt["v"]
                wait("tensor", "v")
                for h in range(NH):
                    pb2 = (h % 2) * 128
                    wait("tensor", "s")
                    emit("tensor", lambda e, h=h, pb2=pb2: e.transpose(
                        ptrb[0:64, pb2:pb2 + 128], attn_sb[:, h * 64:(h + 1) * 64], identbf[:]),
                        incs=[("t", 1)])
                    wait("scalar", "t")
                    emit("scalar", lambda e, h=h, pb2=pb2: e.copy(
                        out=attnT[(h % 2) * 64:(h % 2) * 64 + 64, (h // 2) * 128:(h // 2) * 128 + 128],
                        in_=ptrb[0:64, pb2:pb2 + 128]), incs=[("s", 1)])
                wait("tensor", "s")
                for hf in range(2):
                    for c in range(8):
                        emit("tensor", lambda e, c=c, hf=hf: e.matmul(
                            pbig[:, hf * 512:hf * 512 + 512], attnT[:, c * 128:(c + 1) * 128],
                            wb["ow"][:, c * 1024 + hf * 512:c * 1024 + hf * 512 + 512],
                            start=(c == 0), stop=False))
                    emit("tensor", lambda e, hf=hf: e.matmul(
                        pbig[:, hf * 512:hf * 512 + 512], onesbf[:], obbf[:, hf * 512:hf * 512 + 512],
                        start=False, stop=True), incs=([("t", 1)] if hf == 1 else []))
                wait("vector", "t")
                emit("vector", lambda e, b=b: e.tensor_tensor(out=x3sb[:], in0=pbig[:, 0:1024],
                                                              in1=x2in[b][:], op=ALU.add), incs=[("v", 1)])
                if t >= 1:
                    wait("vector", "ost")
                ffn(x3sb, x4sb[b])
                wait("gpsimd", "v")
                emit("gpsimd", lambda e, t=t, b=b: e.dma_start(
                    out=out.rearrange("(n p) d -> n p d", p=128)[t], in_=x4sb[b][:]), incs=[("ost", 16)])
            wait("gpsimd", "ost", nt * 16)

        block = E(nc.Block())

        @block.sync
        def _(e):
            program("sync")

        @block.scalar
        def _(e):
            program("scalar")

        @block.tensor
        def _(e):
            program("tensor")

        @block.vector
        def _(e):
            program("vector")

        @block.gpsimd
        def _(e):
            program("gpsimd")

    return nc


def _lay(w):
    k, m = w.shape
    return np.ascontiguousarray(w.reshape(k // 128, 128, m).transpose(1, 0, 2).reshape(128, -1))


def host_inputs(inp, S):
    perm32 = np.concatenate([np.arange(0, 32, 2), np.arange(1, 32, 2)])
    wqr = np.asarray(inp["L0_attn_a_W_QR"], np.float32).copy()
    wqr = wqr.reshape(DL, NH, 32)[:, :, perm32].reshape(DL, NH * 32)
    wkr = np.asarray(inp["L0_attn_a_W_KR"], np.float32)[:, perm32]
    d = {
        "wdq": _lay(np.asarray(inp["L0_attn_a_W_DQ"], np.float32)),
        "wuq": _lay(np.asarray(inp["L0_attn_a_W_UQ"], np.float32)),
        "wqr": _lay(wqr),
        "wdkv": _lay(np.asarray(inp["L0_attn_a_W_DKV"], np.float32)),
        "wuk": _lay(np.asarray(inp["L0_attn_a_W_UK"], np.float32)),
        "wuv": _lay(np.asarray(inp["L0_attn_a_W_UV"], np.float32)),
        "wkr": _lay(wkr),
        "wo": _lay(np.asarray(inp["L0_attn_a_W_O"], np.float32)),
        "u1": _lay(np.asarray(inp["L0_ffn_U1"], np.float32)),
        "v1": _lay(np.asarray(inp["L0_ffn_V1"], np.float32)),
        "u2": _lay(np.asarray(inp["L0_ffn_U2"], np.float32)),
        "v2": _lay(np.asarray(inp["L0_ffn_V2"], np.float32)),
        "qw": _lay(np.ascontiguousarray(np.asarray(inp["L0_attn_b_q_w"], np.float32).T)),
        "kw": _lay(np.ascontiguousarray(np.asarray(inp["L0_attn_b_k_w"], np.float32).T)),
        "vw": _lay(np.ascontiguousarray(np.asarray(inp["L0_attn_b_v_w"], np.float32).T)),
        "ow": _lay(np.ascontiguousarray(np.asarray(inp["L0_attn_b_o_w"], np.float32).T)),
        "b1r": np.ascontiguousarray(np.asarray(inp["L0_ffn_b1"], np.float32).reshape(32, 128).T),
        "qbr": np.ascontiguousarray(np.asarray(inp["L0_attn_b_q_b"], np.float32).reshape(16, 64).T),
        "kbr": np.ascontiguousarray(np.asarray(inp["L0_attn_b_k_b"], np.float32).reshape(16, 64).T),
        "b2row": np.asarray(inp["L0_ffn_b2"], np.float32).reshape(1, D),
        "vbrow": np.asarray(inp["L0_attn_b_v_b"], np.float32).reshape(1, D),
        "obrow": np.asarray(inp["L0_attn_b_o_b"], np.float32).reshape(1, D),
    }
    pos = np.arange(S, dtype=np.float32)
    inv = (10000.0 ** (-np.arange(0, 32, 2, dtype=np.float32) / 32.0))
    ang = pos[:, None] * inv[None, :]
    d["ropec"] = np.ascontiguousarray(np.tile(np.cos(ang), (1, 32)).astype(np.float32))
    d["ropes"] = np.ascontiguousarray(np.tile(np.sin(ang), (1, 32)).astype(np.float32))
    ii = np.arange(128)
    d["mlow"] = (ii[:, None] <= ii[None, :]).astype(np.float32)
    d["mhigh"] = (ii[:, None] > ii[None, :]).astype(np.float32)
    d["identbf"] = np.eye(128, dtype=np.float32).astype(ml_dtypes.bfloat16)
    d["onescol"] = np.ones((1, 128), np.float32)
    d["ones128"] = np.ones((128, 1), np.float32)
    d["cst"] = np.tile(np.array(CVALS, np.float32), (128, 1))
    return d


def kernel(**inputs):
    S = 4096
    nt = S // 128
    shared = host_inputs(inputs, S)
    x = np.asarray(inputs["L0_attn_a_x"], np.float32)
    nc = build(nt)
    in_maps = []
    for i in range(8):
        m = dict(shared)
        m["x"] = np.ascontiguousarray(x[i])
        in_maps.append(m)
    res = run_bass_kernel_spmd(nc, in_maps, core_ids=list(range(8)))
    out = np.stack([res.results[i]["out"] for i in range(8)], axis=0)
    return out.astype(np.float32)


# version 02
# speedup vs baseline: 1.0000x; 1.0000x over previous
import math
import numpy as np
import ml_dtypes
from contextlib import ExitStack

import concourse.bass as bass
import concourse.mybir as mybir
from concourse.bass_utils import run_bass_kernel_spmd

F32 = mybir.dt.float32
BF16 = mybir.dt.bfloat16
AF = mybir.ActivationFunctionType
ALU = mybir.AluOpType
AX = mybir.AxisListType

D = 1024
NH = 16
DH = 64
DHR = 32
DL = 256
DFF = 4096
RANK = 256
WIN = 256
EPS = 1e-5
SCL_MLA = 1.0 / math.sqrt(DH + DHR)
SCL_SWA = 1.0 / math.sqrt(DH)
GC1 = 0.7978845608028654
GC2 = 0.7978845608028654 * 0.044715
CVALS = [EPS, -0.5, SCL_MLA, SCL_SWA, 80.0, -80.0, 1.0, -2.0, GC1, GC2, 0.5, 2.0]

ENGS = ("sync", "scalar", "tensor", "vector", "gpsimd")


def build(nt):
    S = nt * 128
    nc = bass.Bass("TRN2", target_bir_lowering=False, debug=False)
    DT = {}

    def din(name, shape, dt=F32):
        DT[name] = nc.dram_tensor(name, shape, dt, kind="ExternalInput").ap()
        return DT[name]

    din("x", [S, D])
    for w, sh in [("wdq", [128, 8 * 256]), ("wuq", [128, 2 * 1024]), ("wqr", [128, 2 * 512]),
                  ("wdkv", [128, 8 * 256]), ("wuk", [128, 2 * 1024]), ("wuv", [128, 2 * 1024]),
                  ("wkr", [128, 8 * 32]), ("wo", [128, 8 * 1024]),
                  ("u1", [128, 8 * 256]), ("v1", [128, 2 * 4096]), ("u2", [128, 32 * 256]),
                  ("v2", [128, 2 * 1024]),
                  ("qw", [128, 8 * 1024]), ("kw", [128, 8 * 1024]), ("vw", [128, 8 * 1024]),
                  ("ow", [128, 8 * 1024])]:
        din(w, sh)
    din("b1r", [128, 32])
    din("qbr", [64, 16])
    din("kbr", [64, 16])
    din("b2row", [1, D])
    din("vbrow", [1, D])
    din("obrow", [1, D])
    din("ropec", [S, 512])
    din("ropes", [S, 512])
    din("mlow", [128, 128])
    din("mhigh", [128, 128])
    din("identbf", [128, 128], BF16)
    din("onescol", [1, 128])
    din("ones128", [128, 1])
    din("cst", [128, len(CVALS)])
    out = nc.dram_tensor("out", [S, D], F32, kind="ExternalOutput").ap()
    kcS = nc.dram_tensor("kcS", [nt, 64, 2048], BF16, kind="Internal").ap()
    krS = nc.dram_tensor("krS", [nt, 32, 2048], BF16, kind="Internal").ap()
    qcS = nc.dram_tensor("qcS", [nt, 64, 2048], BF16, kind="Internal").ap()
    qrS = nc.dram_tensor("qrS", [nt, 32, 2048], BF16, kind="Internal").ap()
    vS = nc.dram_tensor("vS", [nt, 128, NH * 64], BF16, kind="Internal").ap()
    x2S = nc.dram_tensor("x2S", [nt, 128, D], F32, kind="Internal").ap()

    with ExitStack() as stk:
        E = stk.enter_context
        sb = {}

        def T(name, shape, dt=F32):
            sb[name] = E(nc.sbuf_tensor(name, shape, dt))
            return sb[name]

        arena = T("arena", [128, 32768], BF16)

        class ASlice:
            def __init__(self, p0, p1, c0, cols, base=None):
                self.base = arena if base is None else base
                self.p0, self.p1, self.c0, self.cols = p0, p1, c0, cols
                self.shape = (p1 - p0, cols)

            def __getitem__(self, idx):
                ps, cs = idx
                a0 = self.p0 if ps.start is None else self.p0 + ps.start
                a1 = self.p1 if ps.stop is None else self.p0 + ps.stop
                b0 = self.c0 if cs.start is None else self.c0 + cs.start
                b1 = self.c0 + self.cols if cs.stop is None else self.c0 + cs.stop
                return self.base[a0:a1, b0:b1]

        wb = {}
        _off = [0]

        def AV(cols, p0=0, p1=128, at=None):
            c0 = _off[0] if at is None else at
            if at is None:
                _off[0] += cols
            return ASlice(p0, p1, c0, cols)

        for w, cols in [("wdq", 8 * 256), ("wuq", 2 * 1024), ("wqr", 2 * 512), ("wdkv", 8 * 256),
                        ("wuk", 2 * 1024), ("wuv", 2 * 1024), ("wkr", 8 * 32)]:
            wb[w] = AV(cols)
        kcA = AV(2048, 0, 64)
        krA = ASlice(64, 96, kcA.c0, 2048)
        qcA = AV(2048, 0, 64)
        qrA = ASlice(64, 96, qcA.c0, 2048)
        vA = AV(NH * 64)
        wb["wo"] = ASlice(0, 128, 0, 8 * 1024)
        kbuf = [ASlice(0, 96, 8192, 2048), ASlice(0, 96, 10240, 2048)]
        qbuf = ASlice(0, 96, 12288, 2048)
        vbuf = [ASlice(0, 128, 14336, NH * 64), ASlice(0, 128, 14336 + NH * 64, NH * 64)]
        for i, w in enumerate(("qw", "kw", "vw", "ow")):
            wb[w] = ASlice(0, 128, i * 8192, 8 * 1024)
        for w, cols in [("u1", 8 * 256), ("v1", 2 * 4096), ("u2", 32 * 256), ("v2", 2 * 1024)]:
            wb[w] = T("W_" + w, [128, cols], BF16)
        b1r = T("sb_b1r", [128, 32])
        qbr = T("sb_qbr", [64, 16])
        kbr = T("sb_kbr", [64, 16])
        browz = T("browz", [1, D])
        b2bf = T("b2bf", [1, D], BF16)
        vbbf = T("vbbf", [1, D], BF16)
        obbf = T("obbf", [1, D], BF16)
        onesbf = T("onesbf", [1, 128], BF16)
        onecbf = T("onecbf", [128, 1], BF16)
        mlow = T("sb_mlow", [128, 128])
        mhigh = T("sb_mhigh", [128, 128])
        identbf = T("sb_identbf", [128, 128], BF16)
        onescol = T("sb_onescol", [1, 128])
        cstb = T("cstb", [128, len(CVALS)])
        for i, v in enumerate(CVALS):
            nc.const_aps.register(F32, v, cstb[:, i:i + 1])
        stg = [T("stg0", [128, 2048])]

        xa = [T("xa0", [128, D]), T("xa1", [128, D])]
        st12 = T("st12", [128, 12])
        mv = T("mv", [128, 2])
        lnv = T("lnv", [128, 1])
        rstd = T("rstd", [128, 1])
        xnbf = T("xnbf", [128, D], BF16)
        xnT = T("xnT", [128, 8 * 128], BF16)
        rc = [T("rc0", [128, 512])]
        rs = [T("rs0", [128, 512])]
        qrbf = T("qrbf", [128, 512], BF16)
        krbf = T("krbf", [128, 32], BF16)
        krT = T("krT", [32, 128], BF16)
        ckv = T("ckv", [128, 2 * 128], BF16)
        cq = T("cq", [128, 2 * 128], BF16)
        ptb = [T("ptb0", [128, 2048], BF16), T("ptb1", [128, 2048], BF16)]
        attn_sb = T("attn_sb", [128, NH * 64], BF16)
        rec_sb = T("rec_sb", [128, NH])
        attnT = T("attnT", [128, 8 * 128], BF16)
        x1sb = T("x1sb", [128, D])
        x2sb = [T("x2sb0", [128, D]), T("x2sb1", [128, D])]
        ptf = T("ptf", [128, 256], BF16)
        hb = T("hb", [128, 512])
        bufA = T("bufA", [128, 512])
        bufB = T("bufB", [128, 512])
        sq, u0, uu, ee, dd, sg = bufA, bufB, bufA, bufB, bufB, bufA
        ropeA, ropeB = bufA, bufB
        htb = T("htb", [128, 32 * 128], BF16)
        q2t = T("q2t", [128, 256], BF16)

        qTs = T("qTs", [64, NH * 128], BF16)
        kring = T("kring", [64, 3 * NH * 128], BF16)
        vring = T("vring", [128, 3 * NH * 64], BF16)
        x3sb = x1sb
        x4sb = x2sb
        x2in = xa

        pbig = E(nc.psum_tensor("pbig", [128, 2048], F32))
        acc = E(nc.psum_tensor("acc", [128, NH * 64], F32))
        pbf = E(nc.psum_tensor("pbf", [128, 512], BF16))
        pf32 = E(nc.psum_tensor("pf32", [128, 336], F32))
        ptrb = ASlice(0, 64, 0, 256, base=pbf)
        ptrb2 = ASlice(0, 128, 256, 256, base=pbf)
        pq64 = ASlice(0, 64, 0, 256, base=pf32)
        pmisc = ASlice(0, 128, 256, 64, base=pf32)
        pden = ASlice(0, 128, 320, 16, base=pf32)

        sems = {}
        for sname in ("wld", "cv", "xal", "rpl", "t", "s", "v", "gw", "kld0", "kld1", "qld",
                      "xld", "x2st", "x2ld", "ost", "ph"):
            sems[sname] = E(nc.semaphore(sname))

        DT_ones128_f32 = T("ones128f", [128, 1])
        DT_ones128 = DT_ones128_f32[:]

        def program(active):
            eng = getattr(nc, active)
            cnt = {k: 0 for k in sems}

            def emit(engine, f, incs=()):
                r = f(getattr(nc, engine)) if engine == active else None
                for sname, amt in incs:
                    cnt[sname] += amt
                    if r is not None:
                        r.then_inc(sems[sname], amt)
                return cnt

            def wait(engine, sname, val=None):
                v = cnt[sname] if val is None else val
                if v <= 0:
                    return
                if engine == active:
                    eng.wait_ge(sems[sname], v)

            def barrier():
                for sn in ("wld", "xal", "rpl", "kld0", "kld1", "qld", "x2ld"):
                    wait("sync", sn)
                for sn in ("gw", "x2st", "ost"):
                    wait("gpsimd", sn)
                for e in ENGS:
                    if e == active:
                        eng.sem_inc(sems["ph"], 1)
                    cnt["ph"] += 1
                tgt = cnt["ph"]
                for e in ENGS:
                    wait(e, "ph", tgt)

            smalls = [("mlow", mlow), ("mhigh", mhigh), ("identbf", identbf),
                      ("onescol", onescol), ("ones128", DT_ones128_f32), ("cst", cstb),
                      ("b1r", b1r), ("qbr", qbr), ("kbr", kbr)]
            for name, dst in smalls:
                emit("sync", lambda e, n=name, d=dst: e.dma_start(out=d[:], in_=DT[n][:]),
                     incs=[("wld", 16)])
            wait("scalar", "wld")
            emit("scalar", lambda e: e.copy(out=onesbf[:], in_=onescol[:]), incs=[("cv", 1)])
            emit("scalar", lambda e: e.copy(out=onecbf[:], in_=DT_ones128_f32[:]), incs=[("cv", 1)])
            for name, dst in (("b2row", b2bf), ("vbrow", vbbf), ("obrow", obbf)):
                wait("sync", "cv")
                emit("sync", lambda e, n=name: e.dma_start(out=browz[:], in_=DT[n][:]), incs=[("wld", 16)])
                wait("scalar", "wld")
                emit("scalar", lambda e, d=dst: e.copy(out=d[:], in_=browz[:]), incs=[("cv", 1)])
            wconv = [(w, wb[w].shape[1]) for w in ("wdq", "wuq", "wqr", "wdkv", "wuk", "wuv",
                                                   "wkr", "u1", "v1", "u2", "v2")]
            wconv_swa = [(w, 8192) for w in ("qw", "kw", "vw", "ow")]

            def conv_weights(lst):
                si = [0]
                for w, cols in lst:
                    for c0 in range(0, cols, 2048):
                        cw = min(2048, cols - c0)
                        wait("sync", "cv")
                        emit("sync", lambda e, w=w, c0=c0, cw=cw: e.dma_start(
                            out=stg[0][:, 0:cw], in_=DT[w][:, c0:c0 + cw]), incs=[("wld", 16)])
                        wait("scalar", "wld")
                        emit("scalar", lambda e, w=w, c0=c0, cw=cw: e.copy(
                            out=wb[w][:, c0:c0 + cw], in_=stg[0][:, 0:cw]), incs=[("cv", 1)])
                        si[0] += 1

            conv_weights(wconv)
            barrier()

            def layernorm(xin, war_t=None):
                wait("vector", "t", war_t)
                wait("vector", "v")
                emit("vector", lambda e: e.bn_stats(out=st12[:, 0:6], in_=xin[:, 0:512]), incs=[("v", 1)])
                emit("vector", lambda e: e.bn_stats(out=st12[:, 6:12], in_=xin[:, 512:1024]), incs=[("v", 1)])
                wait("vector", "v")
                emit("vector", lambda e: e.bn_aggr(out=mv[:], in_=st12[:]), incs=[("v", 1)])
                wait("scalar", "v")
                emit("scalar", lambda e: e.activation(out=lnv[:], in_=mv[:, 1:2], func=AF.Ln, bias=EPS),
                     incs=[("s", 1)])
                wait("scalar", "s")
                emit("scalar", lambda e: e.activation(out=rstd[:], in_=lnv[:], func=AF.Exp, scale=-0.5),
                     incs=[("s", 1)])
                wait("vector", "s")
                emit("vector", lambda e: e.tensor_scalar(out=xnbf[:], in0=xin[:], scalar1=mv[:, 0:1],
                                                         scalar2=rstd[:], op0=ALU.subtract, op1=ALU.mult),
                     incs=[("v", 1)])

            def transposes_xn():
                for c in range(8):
                    b = c % 2
                    wait("tensor", "v")
                    wait("tensor", "s")
                    emit("tensor", lambda e, c=c, b=b: e.transpose(
                        ptrb2[:, b * 128:(b + 1) * 128], xnbf[:, c * 128:(c + 1) * 128], identbf[:]),
                        incs=[("t", 1)])
                    wait("scalar", "t")
                    emit("scalar", lambda e, c=c, b=b: e.copy(
                        out=xnT[:, c * 128:(c + 1) * 128], in_=ptrb2[:, b * 128:(b + 1) * 128]),
                        incs=[("s", 1)])

            def ffn(xin, xout):
                layernorm(xin)
                transposes_xn()
                wait("tensor", "s")
                for l in range(2):
                    for c in range(8):
                        emit("tensor", lambda e, l=l, c=c: e.matmul(
                            pbig[:, l * 128:(l + 1) * 128],
                            wb["u1"][:, c * 256 + l * 128:c * 256 + (l + 1) * 128],
                            xnT[:, c * 128:(c + 1) * 128],
                            start=(l == 0 and c == 0), stop=(l == 1 and c == 7)),
                            incs=([("t", 1)] if c == 7 else []))
                wait("scalar", "t")
                emit("scalar", lambda e: e.copy(out=ptf[:], in_=pbig[:, 0:256]), incs=[("s", 1)])
                for g in range(8):
                    pb = 1024 + (g % 2) * 512
                    wait("tensor", "s")
                    for jj in range(4):
                        j = g * 4 + jj
                        for l in range(2):
                            emit("tensor", lambda e, j=j, l=l, pb=pb, jj=jj: e.matmul(
                                pbig[:, pb + jj * 128:pb + (jj + 1) * 128],
                                wb["v1"][:, l * 4096 + j * 128:l * 4096 + (j + 1) * 128],
                                ptf[:, l * 128:(l + 1) * 128],
                                start=(jj == 0 and l == 0), stop=(jj == 3 and l == 1)),
                                incs=([("t", 1)] if (jj == 3 and l == 1) else []))
                    wait("scalar", "t")
                    wait("scalar", "v")
                    for jj in range(4):
                        j = g * 4 + jj
                        emit("scalar", lambda e, j=j, jj=jj, pb=pb: e.activation(
                            out=hb[:, jj * 128:(jj + 1) * 128], in_=pbig[:, pb + jj * 128:pb + (jj + 1) * 128],
                            func=AF.Identity, bias=b1r[:, j:j + 1]), incs=[("s", 1)])
                    wait("scalar", "s")
                    emit("scalar", lambda e: e.activation(out=sq[:], in_=hb[:], func=AF.Square),
                         incs=[("s", 1)])
                    wait("vector", "s")
                    emit("vector", lambda e: e.tensor_scalar(out=u0[:], in0=sq[:], scalar1=GC2,
                                                             scalar2=GC1, op0=ALU.mult, op1=ALU.add),
                         incs=[("v", 1)])
                    wait("vector", "v")
                    emit("vector", lambda e: e.tensor_tensor(out=uu[:], in0=u0[:], in1=hb[:], op=ALU.mult),
                         incs=[("v", 1)])
                    wait("scalar", "v")
                    emit("scalar", lambda e: e.activation(out=ee[:], in_=uu[:], func=AF.Exp, scale=-2.0),
                         incs=[("s", 1)])
                    wait("vector", "s")
                    emit("vector", lambda e: e.tensor_scalar_add(out=dd[:], in0=ee[:], scalar1=1.0),
                         incs=[("v", 1)])
                    wait("vector", "v")
                    emit("vector", lambda e: e.reciprocal(out=sg[:], in_=dd[:]), incs=[("v", 1)])
                    wait("vector", "v")
                    emit("vector", lambda e, g=g: e.tensor_tensor(out=htb[:, g * 512:(g + 1) * 512],
                                                                  in0=hb[:], in1=sg[:], op=ALU.mult),
                         incs=[("v", 1)])
                wait("tensor", "v")
                for l in range(2):
                    for j in range(32):
                        emit("tensor", lambda e, l=l, j=j: e.matmul(
                            pbig[:, 256 + l * 128:256 + (l + 1) * 128],
                            wb["u2"][:, j * 256 + l * 128:j * 256 + (l + 1) * 128],
                            htb[:, j * 128:(j + 1) * 128],
                            start=(l == 0 and j == 0), stop=(l == 1 and j == 31)),
                            incs=([("t", 1)] if j == 31 else []))
                wait("scalar", "t")
                emit("scalar", lambda e: e.copy(out=q2t[:], in_=pbig[:, 256:512]), incs=[("s", 1)])
                wait("tensor", "s")
                for hf in range(2):
                    for l in range(2):
                        emit("tensor", lambda e, l=l, hf=hf: e.matmul(
                            pbig[:, 1024 + hf * 512:1536 + hf * 512], q2t[:, l * 128:(l + 1) * 128],
                            wb["v2"][:, l * 1024 + hf * 512:l * 1024 + hf * 512 + 512],
                            start=(l == 0), stop=False))
                    emit("tensor", lambda e, hf=hf: e.matmul(
                        pbig[:, 1024 + hf * 512:1536 + hf * 512], onesbf[:], b2bf[:, hf * 512:hf * 512 + 512],
                        start=False, stop=True), incs=([("t", 1)] if hf == 1 else []))
                wait("vector", "t")
                emit("vector", lambda e, xin=xin, xout=xout: e.tensor_tensor(
                    out=xout[:], in0=pbig[:, 1024:2048], in1=xin[:], op=ALU.add), incs=[("v", 1)])

            xv = DT["x"].rearrange("(n p) d -> n p d", p=128)
            rcv = DT["ropec"].rearrange("(n p) d -> n p d", p=128)
            rsv = DT["ropes"].rearrange("(n p) d -> n p d", p=128)
            for t in range(nt):
                b = t % 2
                if t >= 1:
                    wait("sync", "v")
                emit("sync", lambda e, t=t, b=b: e.dma_start(out=xa[b][:], in_=xv[t]), incs=[("xal", 16)])
                emit("sync", lambda e, t=t: e.dma_start(out=rc[0][:], in_=rcv[t]), incs=[("rpl", 16)])
                emit("sync", lambda e, t=t: e.dma_start(out=rs[0][:], in_=rsv[t]), incs=[("rpl", 16)])
                wait("vector", "xal")
                layernorm(xa[b])
                transposes_xn()
                wait("tensor", "s")
                for i, (wname, dst) in enumerate((("wdkv", ckv), ("wdq", cq))):
                    for l in range(2):
                        for c in range(8):
                            emit("tensor", lambda e, i=i, l=l, c=c, wname=wname: e.matmul(
                                pbig[:, i * 256 + l * 128:i * 256 + (l + 1) * 128],
                                wb[wname][:, c * 256 + l * 128:c * 256 + (l + 1) * 128],
                                xnT[:, c * 128:(c + 1) * 128],
                                start=(i == 0 and l == 0 and c == 0),
                                stop=(i == 1 and l == 1 and c == 7)),
                                incs=([("t", 1)] if c == 7 else []))
                wait("scalar", "t")
                for i, dst in ((0, ckv), (1, cq)):
                    emit("scalar", lambda e, i=i, dst=dst: e.copy(out=dst[:], in_=pbig[:, i * 256:i * 256 + 256]),
                         incs=[("s", 1)])
                if t >= 1:
                    wait("scalar", "gw", t * 80)
                    wait("vector", "gw", t * 80)
                for i, (wname, csrc, dstA) in enumerate((("wuk", ckv, kcA), ("wuq", cq, qcA))):
                    for p in range(8):
                        pb = 512 + i * 256 + (p % 2) * 128
                        wait("tensor", "s")
                        for l in range(2):
                            emit("tensor", lambda e, p=p, l=l, pb=pb, wname=wname, csrc=csrc: e.matmul(
                                pbig[:, pb:pb + 128],
                                wb[wname][:, l * 1024 + p * 128:l * 1024 + (p + 1) * 128],
                                csrc[:, l * 128:(l + 1) * 128],
                                start=(l == 0), stop=(l == 1)), incs=([("t", 1)] if l == 1 else []))
                        wait("scalar", "t")
                        emit("scalar", lambda e, p=p, pb=pb, dstA=dstA: e.copy(
                            out=dstA[0:64, (2 * p) * 128:(2 * p) * 128 + 128], in_=pbig[0:64, pb:pb + 128]),
                            incs=[("s", 1)])
                        emit("scalar", lambda e, p=p, pb=pb, dstA=dstA: e.copy(
                            out=dstA[0:64, (2 * p + 1) * 128:(2 * p + 1) * 128 + 128],
                            in_=pbig[64:128, pb:pb + 128]), incs=[("s", 1)])
                wait("tensor", "s")
                for half in range(2):
                    for l in range(2):
                        emit("tensor", lambda e, l=l, hf=half: e.matmul(
                            pbig[:, 1024 + hf * 512:1536 + hf * 512], ckv[:, l * 128:(l + 1) * 128],
                            wb["wuv"][:, l * 1024 + hf * 512:l * 1024 + hf * 512 + 512],
                            start=(l == 0), stop=(l == 1)),
                            incs=([("t", 1)] if (l == 1 and half == 1) else []))
                wait("scalar", "t")
                emit("scalar", lambda e: e.copy(out=vA[:, 0:NH * 64], in_=pbig[:, 1024:2048]),
                     incs=[("s", 1)])
                wait("tensor", "s")
                for l in range(2):
                    emit("tensor", lambda e, l=l: e.matmul(
                        pbig[:, 0:512], cq[:, l * 128:(l + 1) * 128],
                        wb["wqr"][:, l * 512:(l + 1) * 512],
                        start=(l == 0), stop=(l == 1)), incs=([("t", 1)] if l == 1 else []))
                for c in range(8):
                    emit("tensor", lambda e, c=c: e.matmul(
                        pmisc[:, 0:32], xnT[:, c * 128:(c + 1) * 128],
                        wb["wkr"][:, c * 32:(c + 1) * 32],
                        start=(c == 0), stop=(c == 7)), incs=([("t", 1)] if c == 7 else []))
                wait("vector", "t")
                wait("vector", "rpl")
                emit("vector", lambda e: e.tensor_tensor(out=ropeA[:], in0=pbig[:, 0:512], in1=rc[0][:],
                                                              op=ALU.mult), incs=[("v", 1)])
                emit("vector", lambda e: e.tensor_tensor(out=ropeB[:], in0=pbig[:, 0:512], in1=rs[0][:],
                                                              op=ALU.mult), incs=[("v", 1)])
                wait("vector", "v")
                r3 = lambda ap: ap.rearrange("p (h two f) -> p h two f", two=2, f=16)
                emit("vector", lambda e: e.tensor_tensor(
                    out=r3(qrbf[:])[:, :, 0, :], in0=r3(ropeA[:])[:, :, 0, :],
                    in1=r3(ropeB[:])[:, :, 1, :], op=ALU.subtract), incs=[("v", 1)])
                emit("vector", lambda e: e.tensor_tensor(
                    out=r3(qrbf[:])[:, :, 1, :], in0=r3(ropeB[:])[:, :, 0, :],
                    in1=r3(ropeA[:])[:, :, 1, :], op=ALU.add), incs=[("v", 1)])
                wait("vector", "v")
                emit("vector", lambda e: e.tensor_tensor(out=ropeA[:, 0:32], in0=pmisc[:, 0:32],
                                                              in1=rc[0][:, 0:32], op=ALU.mult), incs=[("v", 1)])
                emit("vector", lambda e: e.tensor_tensor(out=ropeB[:, 0:32], in0=pmisc[:, 0:32],
                                                              in1=rs[0][:, 0:32], op=ALU.mult), incs=[("v", 1)])
                wait("vector", "v")
                emit("vector", lambda e: e.tensor_tensor(out=krbf[:, 0:16], in0=ropeA[:, 0:16],
                                                         in1=ropeB[:, 16:32], op=ALU.subtract), incs=[("v", 1)])
                emit("vector", lambda e: e.tensor_tensor(out=krbf[:, 16:32], in0=ropeB[:, 0:16],
                                                         in1=ropeA[:, 16:32], op=ALU.add), incs=[("v", 1)])
                wait("tensor", "v")
                for h in range(NH):
                    pb2 = (h % 2) * 128
                    wait("tensor", "s")
                    emit("tensor", lambda e, h=h, pb2=pb2: e.transpose(
                        ptrb[0:32, pb2:pb2 + 128], qrbf[:, h * 32:(h + 1) * 32], identbf[:]),
                        incs=[("t", 1)])
                    wait("scalar", "t")
                    emit("scalar", lambda e, h=h, pb2=pb2: e.copy(
                        out=qrA[0:32, h * 128:(h + 1) * 128], in_=ptrb[0:32, pb2:pb2 + 128]),
                        incs=[("s", 1)])
                wait("tensor", "s")
                emit("tensor", lambda e: e.transpose(ptrb[0:32, 0:128], krbf[:], identbf[:]), incs=[("t", 1)])
                wait("scalar", "t")
                emit("scalar", lambda e: e.copy(out=krT[:], in_=ptrb[0:32, 0:128]), incs=[("s", 1)])
                wait("vector", "s")
                for h in range(NH):
                    emit("vector", lambda e, h=h: e.tensor_copy(out=krA[0:32, h * 128:(h + 1) * 128],
                                                                in_=krT[:]),
                         incs=([("v", 1)] if h == NH - 1 else []))
                wait("gpsimd", "s")
                wait("gpsimd", "v")
                emit("gpsimd", lambda e, t=t: e.dma_start(out=kcS[t], in_=kcA[0:64, :]), incs=[("gw", 16)])
                emit("gpsimd", lambda e, t=t: e.dma_start(out=krS[t], in_=krA[0:32, :]), incs=[("gw", 16)])
                emit("gpsimd", lambda e, t=t: e.dma_start(out=qcS[t], in_=qcA[0:64, :]), incs=[("gw", 16)])
                emit("gpsimd", lambda e, t=t: e.dma_start(out=qrS[t], in_=qrA[0:32, :]), incs=[("gw", 16)])
                emit("gpsimd", lambda e, t=t: e.dma_start(out=vS[t], in_=vA[:, 0:NH * 64]), incs=[("gw", 16)])
            barrier()

            conv_weights([("wo", 8192)])
            g = [0]
            pvmark = {}
            scmark = [0]
            finmark = [0]
            kvld = {}
            for t in range(nt):
                b = t % 2
                if t >= 1:
                    wait("sync", "v")
                emit("sync", lambda e, t=t: e.dma_start(out=qbuf[0:64, :], in_=qcS[t]), incs=[("qld", 16)])
                emit("sync", lambda e, t=t: e.dma_start(out=qbuf[64:96, :], in_=qrS[t]), incs=[("qld", 16)])
                emit("sync", lambda e, t=t, b=b: e.dma_start(out=xa[b][:], in_=xv[t]), incs=[("xal", 16)])
                kb0 = g[0] % 2
                sld = "kld%d" % kb0
                wait("sync", "t", pvmark.get(("k", kb0), 0))
                emit("sync", lambda e, kb0=kb0: e.dma_start(out=kbuf[kb0][0:64, :], in_=kcS[0]),
                     incs=[(sld, 16)])
                emit("sync", lambda e, kb0=kb0: e.dma_start(out=kbuf[kb0][64:96, :], in_=krS[0]),
                     incs=[(sld, 16)])
                emit("sync", lambda e, kb0=kb0: e.dma_start(out=vbuf[kb0][:, :], in_=vS[0]),
                     incs=[(sld, 16)])
                kvld[(t, 0)] = (sld, cnt[sld])
                for k in range(t + 1):
                    gk = g[0] + k
                    kb = gk % 2
                    if k + 1 <= t:
                        nb = (gk + 1) % 2
                        snb = "kld%d" % nb
                        wait("sync", "t", pvmark.get(("k", nb), 0))
                        emit("sync", lambda e, k=k, nb=nb: e.dma_start(out=kbuf[nb][0:64, :], in_=kcS[k + 1]),
                             incs=[(snb, 16)])
                        emit("sync", lambda e, k=k, nb=nb: e.dma_start(out=kbuf[nb][64:96, :], in_=krS[k + 1]),
                             incs=[(snb, 16)])
                        emit("sync", lambda e, k=k, nb=nb: e.dma_start(out=vbuf[nb][:, :], in_=vS[k + 1]),
                             incs=[(snb, 16)])
                        kvld[(t, k + 1)] = (snb, cnt[snb])
                    wait("tensor", kvld[(t, k)][0], kvld[(t, k)][1])
                    wait("tensor", "qld")
                    if k == 0:
                        wait("tensor", "v")
                        wait("tensor", "s", scmark[0])
                    for h in range(NH):
                        emit("tensor", lambda e, h=h, kb=kb: e.matmul(
                            pbig[:, h * 128:(h + 1) * 128],
                            kbuf[kb][0:96, h * 128:(h + 1) * 128],
                            qbuf[0:96, h * 128:(h + 1) * 128],
                            start=(h % 4 == 0), stop=(h % 4 == 3)),
                            incs=([("t", 1)] if h == NH - 1 else []))
                    wait("vector", "t")
                    wait("vector", "s", scmark[0])
                    emit("vector", lambda e: e.tensor_scalar(out=pbig[:, 0:2048], in0=pbig[:, 0:2048],
                                                             scalar1=80.0, scalar2=-80.0,
                                                             op0=ALU.min, op1=ALU.max), incs=[("v", 1)])
                    wait("scalar", "v")
                    wait("scalar", "t", pvmark.get(("pt", kb), 0))
                    emit("scalar", lambda e, kb=kb: e.activation(out=ptb[kb][:], in_=pbig[:, 0:2048],
                                                                 func=AF.Exp, scale=SCL_MLA),
                         incs=[("s", 1)])
                    scmark[0] = cnt["s"]
                    if k == t:
                        wait("vector", "s")
                        for h in range(NH):
                            emit("vector", lambda e, h=h, kb=kb: e.tensor_tensor(
                                out=ptb[kb][:, h * 128:(h + 1) * 128], in0=ptb[kb][:, h * 128:(h + 1) * 128],
                                in1=mlow[:], op=ALU.mult), incs=([("v", 1)] if h == NH - 1 else []))
                        wait("tensor", "v")
                    else:
                        wait("tensor", "s")
                    for h in range(NH):
                        emit("tensor", lambda e, h=h, kb=kb, k=k, t=t: e.matmul(
                            acc[:, h * 64:(h + 1) * 64], ptb[kb][:, h * 128:(h + 1) * 128],
                            vbuf[kb][:, h * 64:h * 64 + 64],
                            start=(k == 0 and h % 8 == 0), stop=(k == t and h % 8 == 7)))
                        emit("tensor", lambda e, h=h, kb=kb, k=k, t=t: e.matmul(
                            pden[:, h:h + 1], ptb[kb][:, h * 128:(h + 1) * 128],
                            onecbf[:],
                            start=(k == 0 and h == 0), stop=(k == t and h == NH - 1)),
                            incs=([("t", 1)] if h == NH - 1 else []))
                    pvmark[("pt", kb)] = cnt["t"]
                    pvmark[("k", kb)] = cnt["t"]
                g[0] += t + 1
                wait("vector", "t")
                emit("vector", lambda e: e.reciprocal(out=rec_sb[:, 0:NH], in_=pden[:, 0:NH]),
                     incs=[("v", 1)])
                wait("vector", "v")
                for h in range(NH):
                    emit("vector", lambda e, h=h: e.tensor_scalar(
                        out=attn_sb[:, h * 64:(h + 1) * 64], in0=acc[:, h * 64:h * 64 + 64],
                        scalar1=rec_sb[:, h:h + 1], scalar2=None, op0=ALU.mult),
                        incs=([("v", 1)] if h == NH - 1 else []))
                finmark[0] = cnt["v"]
                wait("tensor", "v")
                for h in range(NH):
                    pb2 = (h % 2) * 128
                    wait("tensor", "s")
                    emit("tensor", lambda e, h=h, pb2=pb2: e.transpose(
                        ptrb[0:64, pb2:pb2 + 128], attn_sb[:, h * 64:(h + 1) * 64], identbf[:]),
                        incs=[("t", 1)])
                    wait("scalar", "t")
                    emit("scalar", lambda e, h=h, pb2=pb2: e.copy(
                        out=attnT[(h % 2) * 64:(h % 2) * 64 + 64, (h // 2) * 128:(h // 2) * 128 + 128],
                        in_=ptrb[0:64, pb2:pb2 + 128]), incs=[("s", 1)])
                wait("tensor", "s")
                for hf in range(2):
                    for c in range(8):
                        emit("tensor", lambda e, c=c, hf=hf: e.matmul(
                            pbig[:, hf * 512:hf * 512 + 512], attnT[:, c * 128:(c + 1) * 128],
                            wb["wo"][:, c * 1024 + hf * 512:c * 1024 + hf * 512 + 512],
                            start=(c == 0), stop=(c == 7)),
                            incs=([("t", 1)] if (c == 7 and hf == 1) else []))
                wait("vector", "t")
                wait("vector", "xal")
                emit("vector", lambda e, b=b: e.tensor_tensor(out=x1sb[:], in0=pbig[:, 0:1024],
                                                              in1=xa[b][:], op=ALU.add), incs=[("v", 1)])
                if t >= 1:
                    wait("vector", "x2st")
                ffn(x1sb, x2sb[b])
                wait("gpsimd", "v")
                emit("gpsimd", lambda e, t=t, b=b: e.dma_start(out=x2S[t], in_=x2sb[b][:]), incs=[("x2st", 16)])
            barrier()

            conv_weights(wconv_swa)
            swa_sc = [0]
            swa_fin = [0]
            ring_rd = {}
            for t in range(nt):
                b = t % 2
                slot = t % 3
                if t >= 1:
                    wait("sync", "v")
                emit("sync", lambda e, t=t, b=b: e.dma_start(out=x2in[b][:], in_=x2S[t]), incs=[("x2ld", 16)])
                wait("vector", "x2ld")
                layernorm(x2in[b])
                transposes_xn()
                for iw, (wname, dst, bias) in enumerate((("qw", qTs, qbr), ("kw", None, kbr))):
                    for h in range(NH):
                        pb2 = (h % 2) * 128
                        wait("tensor", "s")
                        for c in range(8):
                            emit("tensor", lambda e, h=h, c=c, pb2=pb2, wname=wname: e.matmul(
                                pq64[0:64, pb2:pb2 + 128],
                                wb[wname][:, c * 1024 + h * 64:c * 1024 + h * 64 + 64],
                                xnT[:, c * 128:(c + 1) * 128],
                                start=(c == 0), stop=(c == 7)), incs=([("t", 1)] if c == 7 else []))
                        wait("scalar", "t")
                        if iw == 0:
                            emit("scalar", lambda e, h=h, pb2=pb2: e.activation(
                                out=qTs[0:64, h * 128:(h + 1) * 128], in_=pq64[0:64, pb2:pb2 + 128],
                                func=AF.Identity, bias=qbr[0:64, h:h + 1]), incs=[("s", 1)])
                        else:
                            emit("scalar", lambda e, h=h, pb2=pb2, slot=slot: e.activation(
                                out=kring[0:64, slot * 2048 + h * 128:slot * 2048 + (h + 1) * 128],
                                in_=pq64[0:64, pb2:pb2 + 128],
                                func=AF.Identity, bias=kbr[0:64, h:h + 1]), incs=[("s", 1)])
                wait("tensor", "s")
                for hf in range(2):
                    for c in range(8):
                        emit("tensor", lambda e, c=c, hf=hf: e.matmul(
                            pbig[:, hf * 512:hf * 512 + 512], xnT[:, c * 128:(c + 1) * 128],
                            wb["vw"][:, c * 1024 + hf * 512:c * 1024 + hf * 512 + 512],
                            start=(c == 0), stop=False))
                    emit("tensor", lambda e, hf=hf: e.matmul(
                        pbig[:, hf * 512:hf * 512 + 512], onesbf[:], vbbf[:, hf * 512:hf * 512 + 512],
                        start=False, stop=True), incs=([("t", 1)] if hf == 1 else []))
                wait("vector", "t")
                emit("vector", lambda e, slot=slot: e.tensor_copy(
                    out=vring[:, slot * 1024:(slot + 1) * 1024], in_=pbig[:, 0:1024]),
                    incs=[("v", 1)])
                slots = list(range(max(0, t - 2), t + 1))
                for si, k in enumerate(slots):
                    ks = k % 3
                    wait("tensor", "v")
                    wait("tensor", "s")
                    if si == 0:
                        wait("tensor", "v", swa_fin[0])
                        wait("tensor", "s", swa_sc[0])
                    for h in range(NH):
                        emit("tensor", lambda e, h=h, ks=ks: e.matmul(
                            pbig[:, h * 128:(h + 1) * 128],
                            kring[0:64, ks * 2048 + h * 128:ks * 2048 + (h + 1) * 128],
                            qTs[0:64, h * 128:(h + 1) * 128],
                            start=(h % 4 == 0), stop=(h % 4 == 3)), incs=([("t", 1)] if h == NH - 1 else []))
                    pt2 = ptb[si % 2]
                    wait("scalar", "t")
                    wait("scalar", "v", swa_fin[0])
                    wait("scalar", "t", pvmark.get(("spt", si % 2), 0))
                    emit("scalar", lambda e, pt2=pt2: e.activation(out=pt2[:], in_=pbig[:, 0:2048],
                                                                   func=AF.Exp, scale=SCL_SWA),
                         incs=[("s", 1)])
                    swa_sc[0] = cnt["s"]
                    msk = mlow if k == t else (mhigh if k == t - 2 else None)
                    if msk is not None:
                        wait("vector", "s")
                        for h in range(NH):
                            emit("vector", lambda e, h=h, pt2=pt2, msk=msk: e.tensor_tensor(
                                out=pt2[:, h * 128:(h + 1) * 128], in0=pt2[:, h * 128:(h + 1) * 128],
                                in1=msk[:], op=ALU.mult), incs=([("v", 1)] if h == NH - 1 else []))
                        wait("tensor", "v")
                    else:
                        wait("tensor", "s")
                    for h in range(NH):
                        emit("tensor", lambda e, h=h, pt2=pt2, ks=ks, si=si, n=len(slots): e.matmul(
                            acc[:, h * 64:(h + 1) * 64], pt2[:, h * 128:(h + 1) * 128],
                            vring[:, (ks * NH + h) * 64:(ks * NH + h) * 64 + 64],
                            start=(si == 0 and h % 8 == 0), stop=(si == n - 1 and h % 8 == 7)))
                        emit("tensor", lambda e, h=h, pt2=pt2, si=si, n=len(slots): e.matmul(
                            pden[:, h:h + 1], pt2[:, h * 128:(h + 1) * 128],
                            onecbf[:],
                            start=(si == 0 and h == 0), stop=(si == n - 1 and h == NH - 1)),
                            incs=([("t", 1)] if h == NH - 1 else []))
                    pvmark[("spt", si % 2)] = cnt["t"]
                ring_rd[slot] = cnt["t"]
                wait("vector", "t")
                emit("vector", lambda e: e.reciprocal(out=rec_sb[:, 0:NH], in_=pden[:, 0:NH]),
                     incs=[("v", 1)])
                wait("vector", "v")
                for h in range(NH):
                    emit("vector", lambda e, h=h: e.tensor_scalar(
                        out=attn_sb[:, h * 64:(h + 1) * 64], in0=acc[:, h * 64:h * 64 + 64],
                        scalar1=rec_sb[:, h:h + 1], scalar2=None, op0=ALU.mult),
                        incs=([("v", 1)] if h == NH - 1 else []))
                swa_fin[0] = cnt["v"]
                wait("tensor", "v")
                for h in range(NH):
                    pb2 = (h % 2) * 128
                    wait("tensor", "s")
                    emit("tensor", lambda e, h=h, pb2=pb2: e.transpose(
                        ptrb[0:64, pb2:pb2 + 128], attn_sb[:, h * 64:(h + 1) * 64], identbf[:]),
                        incs=[("t", 1)])
                    wait("scalar", "t")
                    emit("scalar", lambda e, h=h, pb2=pb2: e.copy(
                        out=attnT[(h % 2) * 64:(h % 2) * 64 + 64, (h // 2) * 128:(h // 2) * 128 + 128],
                        in_=ptrb[0:64, pb2:pb2 + 128]), incs=[("s", 1)])
                wait("tensor", "s")
                for hf in range(2):
                    for c in range(8):
                        emit("tensor", lambda e, c=c, hf=hf: e.matmul(
                            pbig[:, hf * 512:hf * 512 + 512], attnT[:, c * 128:(c + 1) * 128],
                            wb["ow"][:, c * 1024 + hf * 512:c * 1024 + hf * 512 + 512],
                            start=(c == 0), stop=False))
                    emit("tensor", lambda e, hf=hf: e.matmul(
                        pbig[:, hf * 512:hf * 512 + 512], onesbf[:], obbf[:, hf * 512:hf * 512 + 512],
                        start=False, stop=True), incs=([("t", 1)] if hf == 1 else []))
                wait("vector", "t")
                emit("vector", lambda e, b=b: e.tensor_tensor(out=x3sb[:], in0=pbig[:, 0:1024],
                                                              in1=x2in[b][:], op=ALU.add), incs=[("v", 1)])
                if t >= 1:
                    wait("vector", "ost")
                ffn(x3sb, x4sb[b])
                wait("gpsimd", "v")
                emit("gpsimd", lambda e, t=t, b=b: e.dma_start(
                    out=out.rearrange("(n p) d -> n p d", p=128)[t], in_=x4sb[b][:]), incs=[("ost", 16)])
            wait("gpsimd", "ost", nt * 16)

        block = E(nc.Block())

        @block.sync
        def _(e):
            program("sync")

        @block.scalar
        def _(e):
            program("scalar")

        @block.tensor
        def _(e):
            program("tensor")

        @block.vector
        def _(e):
            program("vector")

        @block.gpsimd
        def _(e):
            program("gpsimd")

    return nc


def _lay(w):
    k, m = w.shape
    return np.ascontiguousarray(w.reshape(k // 128, 128, m).transpose(1, 0, 2).reshape(128, -1))


def host_inputs(inp, S):
    perm32 = np.concatenate([np.arange(0, 32, 2), np.arange(1, 32, 2)])
    wqr = np.asarray(inp["L0_attn_a_W_QR"], np.float32).copy()
    wqr = wqr.reshape(DL, NH, 32)[:, :, perm32].reshape(DL, NH * 32)
    wkr = np.asarray(inp["L0_attn_a_W_KR"], np.float32)[:, perm32]
    d = {
        "wdq": _lay(np.asarray(inp["L0_attn_a_W_DQ"], np.float32)),
        "wuq": _lay(np.asarray(inp["L0_attn_a_W_UQ"], np.float32)),
        "wqr": _lay(wqr),
        "wdkv": _lay(np.asarray(inp["L0_attn_a_W_DKV"], np.float32)),
        "wuk": _lay(np.asarray(inp["L0_attn_a_W_UK"], np.float32)),
        "wuv": _lay(np.asarray(inp["L0_attn_a_W_UV"], np.float32)),
        "wkr": _lay(wkr),
        "wo": _lay(np.asarray(inp["L0_attn_a_W_O"], np.float32)),
        "u1": _lay(np.asarray(inp["L0_ffn_U1"], np.float32)),
        "v1": _lay(np.asarray(inp["L0_ffn_V1"], np.float32)),
        "u2": _lay(np.asarray(inp["L0_ffn_U2"], np.float32)),
        "v2": _lay(np.asarray(inp["L0_ffn_V2"], np.float32)),
        "qw": _lay(np.ascontiguousarray(np.asarray(inp["L0_attn_b_q_w"], np.float32).T)),
        "kw": _lay(np.ascontiguousarray(np.asarray(inp["L0_attn_b_k_w"], np.float32).T)),
        "vw": _lay(np.ascontiguousarray(np.asarray(inp["L0_attn_b_v_w"], np.float32).T)),
        "ow": _lay(np.ascontiguousarray(np.asarray(inp["L0_attn_b_o_w"], np.float32).T)),
        "b1r": np.ascontiguousarray(np.asarray(inp["L0_ffn_b1"], np.float32).reshape(32, 128).T),
        "qbr": np.ascontiguousarray(np.asarray(inp["L0_attn_b_q_b"], np.float32).reshape(16, 64).T),
        "kbr": np.ascontiguousarray(np.asarray(inp["L0_attn_b_k_b"], np.float32).reshape(16, 64).T),
        "b2row": np.asarray(inp["L0_ffn_b2"], np.float32).reshape(1, D),
        "vbrow": np.asarray(inp["L0_attn_b_v_b"], np.float32).reshape(1, D),
        "obrow": np.asarray(inp["L0_attn_b_o_b"], np.float32).reshape(1, D),
    }
    pos = np.arange(S, dtype=np.float32)
    inv = (10000.0 ** (-np.arange(0, 32, 2, dtype=np.float32) / 32.0))
    ang = pos[:, None] * inv[None, :]
    d["ropec"] = np.ascontiguousarray(np.tile(np.cos(ang), (1, 32)).astype(np.float32))
    d["ropes"] = np.ascontiguousarray(np.tile(np.sin(ang), (1, 32)).astype(np.float32))
    ii = np.arange(128)
    d["mlow"] = (ii[:, None] <= ii[None, :]).astype(np.float32)
    d["mhigh"] = (ii[:, None] > ii[None, :]).astype(np.float32)
    d["identbf"] = np.eye(128, dtype=np.float32).astype(ml_dtypes.bfloat16)
    d["onescol"] = np.ones((1, 128), np.float32)
    d["ones128"] = np.ones((128, 1), np.float32)
    d["cst"] = np.tile(np.array(CVALS, np.float32), (128, 1))
    return d


def kernel(**inputs):
    S = 4096
    nt = S // 128
    shared = host_inputs(inputs, S)
    x = np.asarray(inputs["L0_attn_a_x"], np.float32)
    nc = build(nt)
    in_maps = []
    for i in range(8):
        m = dict(shared)
        m["x"] = np.ascontiguousarray(x[i])
        in_maps.append(m)
    res = run_bass_kernel_spmd(nc, in_maps, core_ids=list(range(8)))
    out = np.stack([res.results[i]["out"] for i in range(8)], axis=0)
    return out.astype(np.float32)


# version 03
# speedup vs baseline: 1.0618x; 1.0618x over previous
import math
import numpy as np
import ml_dtypes
from contextlib import ExitStack

import concourse.bass as bass
import concourse.mybir as mybir
from concourse.bass_utils import run_bass_kernel_spmd

F32 = mybir.dt.float32
BF16 = mybir.dt.bfloat16
AF = mybir.ActivationFunctionType
ALU = mybir.AluOpType
AX = mybir.AxisListType

D = 1024
NH = 16
DH = 64
DHR = 32
DL = 256
DFF = 4096
RANK = 256
WIN = 256
EPS = 1e-5
SCL_MLA = 1.0 / math.sqrt(DH + DHR)
SCL_SWA = 1.0 / math.sqrt(DH)
GC1 = 0.7978845608028654
GC2 = 0.7978845608028654 * 0.044715
CVALS = [EPS, -0.5, SCL_MLA, SCL_SWA, 80.0, -80.0, 1.0, -2.0, GC1, GC2, 0.5, 2.0]

ENGS = ("sync", "scalar", "tensor", "vector", "gpsimd")


def build(nt):
    S = nt * 128
    nc = bass.Bass("TRN2", target_bir_lowering=False, debug=False)
    DT = {}

    def din(name, shape, dt=F32):
        DT[name] = nc.dram_tensor(name, shape, dt, kind="ExternalInput").ap()
        return DT[name]

    din("x", [S, D])
    for w, sh in [("wdq", [128, 8 * 256]), ("wuq", [128, 2 * 1024]), ("wqr", [128, 2 * 512]),
                  ("wdkv", [128, 8 * 256]), ("wuk", [128, 2 * 1024]), ("wuv", [128, 2 * 1024]),
                  ("wkr", [128, 8 * 32]), ("wo", [128, 8 * 1024]),
                  ("u1", [128, 8 * 256]), ("v1", [128, 2 * 4096]), ("u2", [128, 32 * 256]),
                  ("v2", [128, 2 * 1024]),
                  ("qw", [128, 8 * 1024]), ("kw", [128, 8 * 1024]), ("vw", [128, 8 * 1024]),
                  ("ow", [128, 8 * 1024])]:
        din(w, sh)
    din("b1r", [128, 32])
    din("qbr", [64, 16])
    din("kbr", [64, 16])
    din("b2row", [1, D])
    din("vbrow", [1, D])
    din("obrow", [1, D])
    din("ropec", [S, 512])
    din("ropes", [S, 512])
    din("mlow", [128, 128])
    din("mhigh", [128, 128])
    din("identbf", [128, 128], BF16)
    din("onescol", [1, 128])
    din("ones128", [128, 1])
    din("cst", [128, len(CVALS)])
    out = nc.dram_tensor("out", [S, D], F32, kind="ExternalOutput").ap()
    kcS = nc.dram_tensor("kcS", [nt, 64, 2048], BF16, kind="Internal").ap()
    krS = nc.dram_tensor("krS", [nt, 32, 2048], BF16, kind="Internal").ap()
    qcS = nc.dram_tensor("qcS", [nt, 64, 2048], BF16, kind="Internal").ap()
    qrS = nc.dram_tensor("qrS", [nt, 32, 2048], BF16, kind="Internal").ap()
    vS = nc.dram_tensor("vS", [nt, 128, NH * 64], BF16, kind="Internal").ap()
    x2S = nc.dram_tensor("x2S", [nt, 128, D], F32, kind="Internal").ap()

    with ExitStack() as stk:
        E = stk.enter_context
        sb = {}

        def T(name, shape, dt=F32):
            sb[name] = E(nc.sbuf_tensor(name, shape, dt))
            return sb[name]

        arena = T("arena", [128, 32768], BF16)

        class ASlice:
            def __init__(self, p0, p1, c0, cols, base=None):
                self.base = arena if base is None else base
                self.p0, self.p1, self.c0, self.cols = p0, p1, c0, cols
                self.shape = (p1 - p0, cols)

            def __getitem__(self, idx):
                ps, cs = idx
                a0 = self.p0 if ps.start is None else self.p0 + ps.start
                a1 = self.p1 if ps.stop is None else self.p0 + ps.stop
                b0 = self.c0 if cs.start is None else self.c0 + cs.start
                b1 = self.c0 + self.cols if cs.stop is None else self.c0 + cs.stop
                return self.base[a0:a1, b0:b1]

        wb = {}
        _off = [0]

        def AV(cols, p0=0, p1=128, at=None):
            c0 = _off[0] if at is None else at
            if at is None:
                _off[0] += cols
            return ASlice(p0, p1, c0, cols)

        for w, cols in [("wdq", 8 * 256), ("wuq", 2 * 1024), ("wqr", 2 * 512), ("wdkv", 8 * 256),
                        ("wuk", 2 * 1024), ("wuv", 2 * 1024), ("wkr", 8 * 32)]:
            wb[w] = AV(cols)
        kcA = AV(2048, 0, 64)
        krA = ASlice(64, 96, kcA.c0, 2048)
        qcA = AV(2048, 0, 64)
        qrA = ASlice(64, 96, qcA.c0, 2048)
        vA = AV(NH * 64)
        wb["wo"] = ASlice(0, 128, 0, 8 * 1024)
        kbuf = [ASlice(0, 96, 8192, 2048), ASlice(0, 96, 10240, 2048)]
        qbuf = ASlice(0, 96, 12288, 2048)
        vbuf = [ASlice(0, 128, 14336, NH * 64), ASlice(0, 128, 14336 + NH * 64, NH * 64)]
        for i, w in enumerate(("qw", "kw", "vw", "ow")):
            wb[w] = ASlice(0, 128, i * 8192, 8 * 1024)
        for w, cols in [("u1", 8 * 256), ("v1", 2 * 4096), ("u2", 32 * 256), ("v2", 2 * 1024)]:
            wb[w] = T("W_" + w, [128, cols], BF16)
        b1r = T("sb_b1r", [128, 32])
        qbr = T("sb_qbr", [64, 16])
        kbr = T("sb_kbr", [64, 16])
        browz = T("browz", [1, D])
        b2bf = T("b2bf", [1, D], BF16)
        vbbf = T("vbbf", [1, D], BF16)
        obbf = T("obbf", [1, D], BF16)
        onesbf = T("onesbf", [1, 128], BF16)
        onecbf = T("onecbf", [128, 1], BF16)
        mlow = T("sb_mlow", [128, 128])
        mhigh = T("sb_mhigh", [128, 128])
        identbf = T("sb_identbf", [128, 128], BF16)
        onescol = T("sb_onescol", [1, 128])
        cstb = T("cstb", [128, len(CVALS)])
        for i, v in enumerate(CVALS):
            nc.const_aps.register(F32, v, cstb[:, i:i + 1])
        stg = [T("stg0", [128, 2048])]

        xa = [T("xa0", [128, D]), T("xa1", [128, D])]
        st12 = T("st12", [128, 12])
        mv = T("mv", [128, 2])
        lnv = T("lnv", [128, 1])
        rstd = T("rstd", [128, 1])
        xnbf = T("xnbf", [128, D], BF16)
        xnT = T("xnT", [128, 8 * 128], BF16)
        rc = [T("rc0", [128, 512])]
        rs = [T("rs0", [128, 512])]
        qrbf = T("qrbf", [128, 512], BF16)
        krbf = T("krbf", [128, 32], BF16)
        krT = T("krT", [32, 128], BF16)
        ckv = T("ckv", [128, 2 * 128], BF16)
        cq = T("cq", [128, 2 * 128], BF16)
        ptb = [T("ptb0", [128, 2048], BF16), T("ptb1", [128, 2048], BF16)]
        attn_sb = T("attn_sb", [128, NH * 64], BF16)
        rec_sb = T("rec_sb", [128, NH])
        attnT = T("attnT", [128, 8 * 128], BF16)
        x1sb = T("x1sb", [128, D])
        x2sb = [T("x2sb0", [128, D]), T("x2sb1", [128, D])]
        ptf = T("ptf", [128, 256], BF16)
        hb = T("hb", [128, 512])
        bufA = T("bufA", [128, 512])
        bufB = T("bufB", [128, 512])
        sq, u0, uu, ee, dd, sg = bufA, bufB, bufA, bufB, bufB, bufA
        ropeA, ropeB = bufA, bufB
        htb = T("htb", [128, 32 * 128], BF16)
        q2t = T("q2t", [128, 256], BF16)

        qTs = T("qTs", [64, NH * 128], BF16)
        kring = T("kring", [64, 3 * NH * 128], BF16)
        vring = T("vring", [128, 3 * NH * 64], BF16)
        x3sb = x1sb
        x4sb = x2sb
        x2in = xa

        pbig = E(nc.psum_tensor("pbig", [128, 2048], F32))
        acc = E(nc.psum_tensor("acc", [128, NH * 64], F32))
        pbf = E(nc.psum_tensor("pbf", [128, 512], BF16))
        pf32 = E(nc.psum_tensor("pf32", [128, 336], F32))
        ptrb = ASlice(0, 64, 0, 256, base=pbf)
        ptrb2 = ASlice(0, 128, 256, 256, base=pbf)
        pq64 = ASlice(0, 64, 0, 256, base=pf32)
        pmisc = ASlice(0, 128, 256, 64, base=pf32)
        pden = ASlice(0, 128, 320, 16, base=pf32)

        sems = {}
        for sname in ("wld", "cv", "xal", "rpl", "t", "s", "v", "gw", "kld0", "kld1", "qld",
                      "xld", "x2st", "x2ld", "ost", "ph"):
            sems[sname] = E(nc.semaphore(sname))

        DT_ones128_f32 = T("ones128f", [128, 1])
        DT_ones128 = DT_ones128_f32[:]

        def program(active):
            eng = getattr(nc, active)
            cnt = {k: 0 for k in sems}

            def emit(engine, f, incs=()):
                r = f(getattr(nc, engine)) if engine == active else None
                for sname, amt in incs:
                    cnt[sname] += amt
                    if r is not None:
                        r.then_inc(sems[sname], amt)
                return cnt

            def wait(engine, sname, val=None):
                v = cnt[sname] if val is None else val
                if v <= 0:
                    return
                if engine == active:
                    eng.wait_ge(sems[sname], v)

            def barrier():
                for sn in ("wld", "xal", "rpl", "kld0", "kld1", "qld", "x2ld"):
                    wait("sync", sn)
                for sn in ("gw", "x2st", "ost"):
                    wait("gpsimd", sn)
                for e in ENGS:
                    if e == active:
                        eng.sem_inc(sems["ph"], 1)
                    cnt["ph"] += 1
                tgt = cnt["ph"]
                for e in ENGS:
                    wait(e, "ph", tgt)

            smalls = [("mlow", mlow), ("mhigh", mhigh), ("identbf", identbf),
                      ("onescol", onescol), ("ones128", DT_ones128_f32), ("cst", cstb),
                      ("b1r", b1r), ("qbr", qbr), ("kbr", kbr)]
            for name, dst in smalls:
                emit("sync", lambda e, n=name, d=dst: e.dma_start(out=d[:], in_=DT[n][:]),
                     incs=[("wld", 16)])
            wait("scalar", "wld")
            emit("scalar", lambda e: e.copy(out=onesbf[:], in_=onescol[:]), incs=[("cv", 1)])
            emit("scalar", lambda e: e.copy(out=onecbf[:], in_=DT_ones128_f32[:]), incs=[("cv", 1)])
            for name, dst in (("b2row", b2bf), ("vbrow", vbbf), ("obrow", obbf)):
                wait("sync", "cv")
                emit("sync", lambda e, n=name: e.dma_start(out=browz[:], in_=DT[n][:]), incs=[("wld", 16)])
                wait("scalar", "wld")
                emit("scalar", lambda e, d=dst: e.copy(out=d[:], in_=browz[:]), incs=[("cv", 1)])
            wconv = [(w, wb[w].shape[1]) for w in ("wdq", "wuq", "wqr", "wdkv", "wuk", "wuv",
                                                   "wkr", "u1", "v1", "u2", "v2")]
            wconv_swa = [(w, 8192) for w in ("qw", "kw", "vw", "ow")]

            def conv_weights(lst):
                si = [0]
                for w, cols in lst:
                    for c0 in range(0, cols, 2048):
                        cw = min(2048, cols - c0)
                        wait("sync", "cv")
                        emit("sync", lambda e, w=w, c0=c0, cw=cw: e.dma_start(
                            out=stg[0][:, 0:cw], in_=DT[w][:, c0:c0 + cw]), incs=[("wld", 16)])
                        wait("scalar", "wld")
                        emit("scalar", lambda e, w=w, c0=c0, cw=cw: e.copy(
                            out=wb[w][:, c0:c0 + cw], in_=stg[0][:, 0:cw]), incs=[("cv", 1)])
                        si[0] += 1

            conv_weights(wconv)
            barrier()

            def layernorm(xin, war_t=None):
                wait("vector", "t", war_t)
                wait("vector", "v")
                emit("vector", lambda e: e.bn_stats(out=st12[:, 0:6], in_=xin[:, 0:512]), incs=[("v", 1)])
                emit("vector", lambda e: e.bn_stats(out=st12[:, 6:12], in_=xin[:, 512:1024]), incs=[("v", 1)])
                wait("vector", "v")
                emit("vector", lambda e: e.bn_aggr(out=mv[:], in_=st12[:]), incs=[("v", 1)])
                wait("scalar", "v")
                emit("scalar", lambda e: e.activation(out=lnv[:], in_=mv[:, 1:2], func=AF.Ln, bias=EPS),
                     incs=[("s", 1)])
                wait("scalar", "s")
                emit("scalar", lambda e: e.activation(out=rstd[:], in_=lnv[:], func=AF.Exp, scale=-0.5),
                     incs=[("s", 1)])
                wait("vector", "s")
                emit("vector", lambda e: e.tensor_scalar(out=xnbf[:], in0=xin[:], scalar1=mv[:, 0:1],
                                                         scalar2=rstd[:], op0=ALU.subtract, op1=ALU.mult),
                     incs=[("v", 1)])

            def transposes_xn():
                for c in range(8):
                    b = c % 2
                    wait("tensor", "v")
                    wait("tensor", "s")
                    emit("tensor", lambda e, c=c, b=b: e.transpose(
                        ptrb2[:, b * 128:(b + 1) * 128], xnbf[:, c * 128:(c + 1) * 128], identbf[:]),
                        incs=[("t", 1)])
                    wait("scalar", "t")
                    emit("scalar", lambda e, c=c, b=b: e.copy(
                        out=xnT[:, c * 128:(c + 1) * 128], in_=ptrb2[:, b * 128:(b + 1) * 128]),
                        incs=[("s", 1)])

            def ffn(xin, xout):
                layernorm(xin)
                transposes_xn()
                wait("tensor", "s")
                for l in range(2):
                    for c in range(8):
                        emit("tensor", lambda e, l=l, c=c: e.matmul(
                            pbig[:, l * 128:(l + 1) * 128],
                            wb["u1"][:, c * 256 + l * 128:c * 256 + (l + 1) * 128],
                            xnT[:, c * 128:(c + 1) * 128],
                            start=(l == 0 and c == 0), stop=(l == 1 and c == 7)),
                            incs=([("t", 1)] if c == 7 else []))
                wait("scalar", "t")
                emit("scalar", lambda e: e.copy(out=ptf[:], in_=pbig[:, 0:256]), incs=[("s", 1)])
                for g in range(8):
                    pb = 1024 + (g % 2) * 512
                    wait("tensor", "s")
                    for jj in range(4):
                        j = g * 4 + jj
                        for l in range(2):
                            emit("tensor", lambda e, j=j, l=l, pb=pb, jj=jj: e.matmul(
                                pbig[:, pb + jj * 128:pb + (jj + 1) * 128],
                                wb["v1"][:, l * 4096 + j * 128:l * 4096 + (j + 1) * 128],
                                ptf[:, l * 128:(l + 1) * 128],
                                start=(jj == 0 and l == 0), stop=(jj == 3 and l == 1)),
                                incs=([("t", 1)] if (jj == 3 and l == 1) else []))
                    wait("scalar", "t")
                    wait("scalar", "v")
                    for jj in range(4):
                        j = g * 4 + jj
                        emit("scalar", lambda e, j=j, jj=jj, pb=pb: e.activation(
                            out=hb[:, jj * 128:(jj + 1) * 128], in_=pbig[:, pb + jj * 128:pb + (jj + 1) * 128],
                            func=AF.Identity, bias=b1r[:, j:j + 1]), incs=[("s", 1)])
                    wait("scalar", "s")
                    emit("scalar", lambda e: e.activation(out=sq[:], in_=hb[:], func=AF.Square),
                         incs=[("s", 1)])
                    wait("vector", "s")
                    emit("vector", lambda e: e.tensor_scalar(out=u0[:], in0=sq[:], scalar1=GC2,
                                                             scalar2=GC1, op0=ALU.mult, op1=ALU.add),
                         incs=[("v", 1)])
                    wait("vector", "v")
                    emit("vector", lambda e: e.tensor_tensor(out=uu[:], in0=u0[:], in1=hb[:], op=ALU.mult),
                         incs=[("v", 1)])
                    wait("scalar", "v")
                    emit("scalar", lambda e: e.activation(out=ee[:], in_=uu[:], func=AF.Exp, scale=-2.0),
                         incs=[("s", 1)])
                    wait("vector", "s")
                    emit("vector", lambda e: e.tensor_scalar_add(out=dd[:], in0=ee[:], scalar1=1.0),
                         incs=[("v", 1)])
                    wait("vector", "v")
                    emit("vector", lambda e: e.reciprocal(out=sg[:], in_=dd[:]), incs=[("v", 1)])
                    wait("vector", "v")
                    emit("vector", lambda e, g=g: e.tensor_tensor(out=htb[:, g * 512:(g + 1) * 512],
                                                                  in0=hb[:], in1=sg[:], op=ALU.mult),
                         incs=[("v", 1)])
                wait("tensor", "v")
                for l in range(2):
                    for j in range(32):
                        emit("tensor", lambda e, l=l, j=j: e.matmul(
                            pbig[:, 256 + l * 128:256 + (l + 1) * 128],
                            wb["u2"][:, j * 256 + l * 128:j * 256 + (l + 1) * 128],
                            htb[:, j * 128:(j + 1) * 128],
                            start=(l == 0 and j == 0), stop=(l == 1 and j == 31)),
                            incs=([("t", 1)] if j == 31 else []))
                wait("scalar", "t")
                emit("scalar", lambda e: e.copy(out=q2t[:], in_=pbig[:, 256:512]), incs=[("s", 1)])
                wait("tensor", "s")
                for hf in range(2):
                    for l in range(2):
                        emit("tensor", lambda e, l=l, hf=hf: e.matmul(
                            pbig[:, 1024 + hf * 512:1536 + hf * 512], q2t[:, l * 128:(l + 1) * 128],
                            wb["v2"][:, l * 1024 + hf * 512:l * 1024 + hf * 512 + 512],
                            start=(l == 0), stop=False))
                    emit("tensor", lambda e, hf=hf: e.matmul(
                        pbig[:, 1024 + hf * 512:1536 + hf * 512], onesbf[:], b2bf[:, hf * 512:hf * 512 + 512],
                        start=False, stop=True), incs=([("t", 1)] if hf == 1 else []))
                wait("vector", "t")
                emit("vector", lambda e, xin=xin, xout=xout: e.tensor_tensor(
                    out=xout[:], in0=pbig[:, 1024:2048], in1=xin[:], op=ALU.add), incs=[("v", 1)])

            xv = DT["x"].rearrange("(n p) d -> n p d", p=128)
            rcv = DT["ropec"].rearrange("(n p) d -> n p d", p=128)
            rsv = DT["ropes"].rearrange("(n p) d -> n p d", p=128)
            for t in range(nt):
                b = t % 2
                if t >= 1:
                    wait("sync", "v")
                emit("sync", lambda e, t=t, b=b: e.dma_start(out=xa[b][:], in_=xv[t]), incs=[("xal", 16)])
                emit("sync", lambda e, t=t: e.dma_start(out=rc[0][:], in_=rcv[t]), incs=[("rpl", 16)])
                emit("sync", lambda e, t=t: e.dma_start(out=rs[0][:], in_=rsv[t]), incs=[("rpl", 16)])
                wait("vector", "xal")
                layernorm(xa[b])
                transposes_xn()
                wait("tensor", "s")
                for i, (wname, dst) in enumerate((("wdkv", ckv), ("wdq", cq))):
                    for l in range(2):
                        for c in range(8):
                            emit("tensor", lambda e, i=i, l=l, c=c, wname=wname: e.matmul(
                                pbig[:, i * 256 + l * 128:i * 256 + (l + 1) * 128],
                                wb[wname][:, c * 256 + l * 128:c * 256 + (l + 1) * 128],
                                xnT[:, c * 128:(c + 1) * 128],
                                start=(i == 0 and l == 0 and c == 0),
                                stop=(i == 1 and l == 1 and c == 7)),
                                incs=([("t", 1)] if c == 7 else []))
                wait("scalar", "t")
                for i, dst in ((0, ckv), (1, cq)):
                    emit("scalar", lambda e, i=i, dst=dst: e.copy(out=dst[:], in_=pbig[:, i * 256:i * 256 + 256]),
                         incs=[("s", 1)])
                if t >= 1:
                    wait("scalar", "gw", t * 80)
                    wait("vector", "gw", t * 80)
                for i, (wname, csrc, dstA) in enumerate((("wuk", ckv, kcA), ("wuq", cq, qcA))):
                    for p in range(8):
                        pb = 512 + i * 256 + (p % 2) * 128
                        wait("tensor", "s")
                        for l in range(2):
                            emit("tensor", lambda e, p=p, l=l, pb=pb, wname=wname, csrc=csrc: e.matmul(
                                pbig[:, pb:pb + 128],
                                wb[wname][:, l * 1024 + p * 128:l * 1024 + (p + 1) * 128],
                                csrc[:, l * 128:(l + 1) * 128],
                                start=(l == 0), stop=(l == 1)), incs=([("t", 1)] if l == 1 else []))
                        wait("scalar", "t")
                        emit("scalar", lambda e, p=p, pb=pb, dstA=dstA: e.copy(
                            out=dstA[0:64, (2 * p) * 128:(2 * p) * 128 + 128], in_=pbig[0:64, pb:pb + 128]),
                            incs=[("s", 1)])
                        emit("scalar", lambda e, p=p, pb=pb, dstA=dstA: e.copy(
                            out=dstA[0:64, (2 * p + 1) * 128:(2 * p + 1) * 128 + 128],
                            in_=pbig[64:128, pb:pb + 128]), incs=[("s", 1)])
                wait("tensor", "s")
                for half in range(2):
                    for l in range(2):
                        emit("tensor", lambda e, l=l, hf=half: e.matmul(
                            pbig[:, 1024 + hf * 512:1536 + hf * 512], ckv[:, l * 128:(l + 1) * 128],
                            wb["wuv"][:, l * 1024 + hf * 512:l * 1024 + hf * 512 + 512],
                            start=(l == 0), stop=(l == 1)),
                            incs=([("t", 1)] if (l == 1 and half == 1) else []))
                wait("scalar", "t")
                emit("scalar", lambda e: e.copy(out=vA[:, 0:NH * 64], in_=pbig[:, 1024:2048]),
                     incs=[("s", 1)])
                wait("tensor", "s")
                for l in range(2):
                    emit("tensor", lambda e, l=l: e.matmul(
                        pbig[:, 0:512], cq[:, l * 128:(l + 1) * 128],
                        wb["wqr"][:, l * 512:(l + 1) * 512],
                        start=(l == 0), stop=(l == 1)), incs=([("t", 1)] if l == 1 else []))
                for c in range(8):
                    emit("tensor", lambda e, c=c: e.matmul(
                        pmisc[:, 0:32], xnT[:, c * 128:(c + 1) * 128],
                        wb["wkr"][:, c * 32:(c + 1) * 32],
                        start=(c == 0), stop=(c == 7)), incs=([("t", 1)] if c == 7 else []))
                wait("vector", "t")
                wait("vector", "rpl")
                emit("vector", lambda e: e.tensor_tensor(out=ropeA[:], in0=pbig[:, 0:512], in1=rc[0][:],
                                                              op=ALU.mult), incs=[("v", 1)])
                emit("vector", lambda e: e.tensor_tensor(out=ropeB[:], in0=pbig[:, 0:512], in1=rs[0][:],
                                                              op=ALU.mult), incs=[("v", 1)])
                wait("vector", "v")
                r3 = lambda ap: ap.rearrange("p (h two f) -> p h two f", two=2, f=16)
                emit("vector", lambda e: e.tensor_tensor(
                    out=r3(qrbf[:])[:, :, 0, :], in0=r3(ropeA[:])[:, :, 0, :],
                    in1=r3(ropeB[:])[:, :, 1, :], op=ALU.subtract), incs=[("v", 1)])
                emit("vector", lambda e: e.tensor_tensor(
                    out=r3(qrbf[:])[:, :, 1, :], in0=r3(ropeB[:])[:, :, 0, :],
                    in1=r3(ropeA[:])[:, :, 1, :], op=ALU.add), incs=[("v", 1)])
                wait("vector", "v")
                emit("vector", lambda e: e.tensor_tensor(out=ropeA[:, 0:32], in0=pmisc[:, 0:32],
                                                              in1=rc[0][:, 0:32], op=ALU.mult), incs=[("v", 1)])
                emit("vector", lambda e: e.tensor_tensor(out=ropeB[:, 0:32], in0=pmisc[:, 0:32],
                                                              in1=rs[0][:, 0:32], op=ALU.mult), incs=[("v", 1)])
                wait("vector", "v")
                emit("vector", lambda e: e.tensor_tensor(out=krbf[:, 0:16], in0=ropeA[:, 0:16],
                                                         in1=ropeB[:, 16:32], op=ALU.subtract), incs=[("v", 1)])
                emit("vector", lambda e: e.tensor_tensor(out=krbf[:, 16:32], in0=ropeB[:, 0:16],
                                                         in1=ropeA[:, 16:32], op=ALU.add), incs=[("v", 1)])
                wait("tensor", "v")
                for h in range(NH):
                    pb2 = (h % 2) * 128
                    wait("tensor", "s")
                    emit("tensor", lambda e, h=h, pb2=pb2: e.transpose(
                        ptrb[0:32, pb2:pb2 + 128], qrbf[:, h * 32:(h + 1) * 32], identbf[:]),
                        incs=[("t", 1)])
                    wait("scalar", "t")
                    emit("scalar", lambda e, h=h, pb2=pb2: e.copy(
                        out=qrA[0:32, h * 128:(h + 1) * 128], in_=ptrb[0:32, pb2:pb2 + 128]),
                        incs=[("s", 1)])
                wait("tensor", "s")
                emit("tensor", lambda e: e.transpose(ptrb[0:32, 0:128], krbf[:], identbf[:]), incs=[("t", 1)])
                wait("scalar", "t")
                emit("scalar", lambda e: e.copy(out=krT[:], in_=ptrb[0:32, 0:128]), incs=[("s", 1)])
                wait("vector", "s")
                for h in range(NH):
                    emit("vector", lambda e, h=h: e.tensor_copy(out=krA[0:32, h * 128:(h + 1) * 128],
                                                                in_=krT[:]),
                         incs=([("v", 1)] if h == NH - 1 else []))
                wait("gpsimd", "s")
                wait("gpsimd", "v")
                emit("gpsimd", lambda e, t=t: e.dma_start(out=kcS[t], in_=kcA[0:64, :]), incs=[("gw", 16)])
                emit("gpsimd", lambda e, t=t: e.dma_start(out=krS[t], in_=krA[0:32, :]), incs=[("gw", 16)])
                emit("gpsimd", lambda e, t=t: e.dma_start(out=qcS[t], in_=qcA[0:64, :]), incs=[("gw", 16)])
                emit("gpsimd", lambda e, t=t: e.dma_start(out=qrS[t], in_=qrA[0:32, :]), incs=[("gw", 16)])
                emit("gpsimd", lambda e, t=t: e.dma_start(out=vS[t], in_=vA[:, 0:NH * 64]), incs=[("gw", 16)])
            barrier()

            conv_weights([("wo", 8192)])
            g = [0]
            pvmark = {}
            scmark = {0: 0, 1: 0}
            finmark = [0]
            kvld = {}
            for t in range(nt):
                b = t % 2
                if t >= 1:
                    wait("sync", "v")
                emit("sync", lambda e, t=t: e.dma_start(out=qbuf[0:64, :], in_=qcS[t]), incs=[("qld", 16)])
                emit("sync", lambda e, t=t: e.dma_start(out=qbuf[64:96, :], in_=qrS[t]), incs=[("qld", 16)])
                emit("sync", lambda e, t=t, b=b: e.dma_start(out=xa[b][:], in_=xv[t]), incs=[("xal", 16)])
                kb0 = g[0] % 2
                sld = "kld%d" % kb0
                wait("sync", "t", pvmark.get(("k", kb0), 0))
                emit("sync", lambda e, kb0=kb0: e.dma_start(out=kbuf[kb0][0:64, :], in_=kcS[0]),
                     incs=[(sld, 16)])
                emit("sync", lambda e, kb0=kb0: e.dma_start(out=kbuf[kb0][64:96, :], in_=krS[0]),
                     incs=[(sld, 16)])
                emit("sync", lambda e, kb0=kb0: e.dma_start(out=vbuf[kb0][:, :], in_=vS[0]),
                     incs=[(sld, 16)])
                kvld[(t, 0)] = (sld, cnt[sld])
                for k in range(t + 1):
                    gk = g[0] + k
                    kb = gk % 2
                    if k + 1 <= t:
                        nb = (gk + 1) % 2
                        snb = "kld%d" % nb
                        wait("sync", "t", pvmark.get(("k", nb), 0))
                        emit("sync", lambda e, k=k, nb=nb: e.dma_start(out=kbuf[nb][0:64, :], in_=kcS[k + 1]),
                             incs=[(snb, 16)])
                        emit("sync", lambda e, k=k, nb=nb: e.dma_start(out=kbuf[nb][64:96, :], in_=krS[k + 1]),
                             incs=[(snb, 16)])
                        emit("sync", lambda e, k=k, nb=nb: e.dma_start(out=vbuf[nb][:, :], in_=vS[k + 1]),
                             incs=[(snb, 16)])
                        kvld[(t, k + 1)] = (snb, cnt[snb])
                    wait("tensor", kvld[(t, k)][0], kvld[(t, k)][1])
                    wait("tensor", "qld")
                    if k == 0:
                        wait("tensor", "v")
                    expm = {}
                    for grp in range(2):
                        h0 = grp * 8
                        wait("tensor", "s", scmark[grp] if isinstance(scmark, dict) else scmark[0])
                        for hh in range(8):
                            h = h0 + hh
                            emit("tensor", lambda e, h=h, kb=kb: e.matmul(
                                pbig[:, h * 128:(h + 1) * 128],
                                kbuf[kb][0:96, h * 128:(h + 1) * 128],
                                qbuf[0:96, h * 128:(h + 1) * 128],
                                start=(h % 4 == 0), stop=(h % 4 == 3)),
                                incs=([("t", 1)] if hh == 7 else []))
                        scm = cnt["t"]
                        wait("vector", "t", scm)
                        wait("vector", "s", scmark[grp] if isinstance(scmark, dict) else scmark[0])
                        emit("vector", lambda e, h0=h0: e.tensor_scalar(
                            out=pbig[:, h0 * 128:(h0 + 8) * 128], in0=pbig[:, h0 * 128:(h0 + 8) * 128],
                            scalar1=80.0, scalar2=-80.0, op0=ALU.min, op1=ALU.max), incs=[("v", 1)])
                        clm = cnt["v"]
                        wait("scalar", "v", clm)
                        wait("scalar", "t", pvmark.get(("pt", kb, grp), 0))
                        emit("scalar", lambda e, kb=kb, h0=h0: e.activation(
                            out=ptb[kb][:, h0 * 128:(h0 + 8) * 128], in_=pbig[:, h0 * 128:(h0 + 8) * 128],
                            func=AF.Exp, scale=SCL_MLA), incs=[("s", 1)])
                        scmark[grp] = cnt["s"]
                        expm[grp] = cnt["s"]
                        if k == t:
                            wait("vector", "s", expm[grp])
                            for hh in range(8):
                                h = h0 + hh
                                emit("vector", lambda e, h=h, kb=kb: e.tensor_tensor(
                                    out=ptb[kb][:, h * 128:(h + 1) * 128],
                                    in0=ptb[kb][:, h * 128:(h + 1) * 128],
                                    in1=mlow[:], op=ALU.mult), incs=([("v", 1)] if hh == 7 else []))
                            expm[(grp, "v")] = cnt["v"]
                    for grp in range(2):
                        h0 = grp * 8
                        if k == t:
                            wait("tensor", "v", expm[(grp, "v")])
                        else:
                            wait("tensor", "s", expm[grp])
                        for hh in range(8):
                            h = h0 + hh
                            emit("tensor", lambda e, h=h, kb=kb, k=k, t=t: e.matmul(
                                acc[:, h * 64:(h + 1) * 64], ptb[kb][:, h * 128:(h + 1) * 128],
                                vbuf[kb][:, h * 64:h * 64 + 64],
                                start=(k == 0 and h % 8 == 0), stop=(k == t and h % 8 == 7)))
                            emit("tensor", lambda e, h=h, kb=kb, k=k, t=t: e.matmul(
                                pden[:, h:h + 1], ptb[kb][:, h * 128:(h + 1) * 128],
                                onecbf[:],
                                start=(k == 0 and h == 0), stop=(k == t and h == NH - 1)),
                                incs=([("t", 1)] if (hh == 7) else []))
                        pvmark[("pt", kb, grp)] = cnt["t"]
                    pvmark[("k", kb)] = cnt["t"]
                g[0] += t + 1
                wait("vector", "t")
                emit("vector", lambda e: e.reciprocal(out=rec_sb[:, 0:NH], in_=pden[:, 0:NH]),
                     incs=[("v", 1)])
                wait("vector", "v")
                for h in range(NH):
                    emit("vector", lambda e, h=h: e.tensor_scalar(
                        out=attn_sb[:, h * 64:(h + 1) * 64], in0=acc[:, h * 64:h * 64 + 64],
                        scalar1=rec_sb[:, h:h + 1], scalar2=None, op0=ALU.mult),
                        incs=([("v", 1)] if h == NH - 1 else []))
                finmark[0] = cnt["v"]
                wait("tensor", "v")
                for h in range(NH):
                    pb2 = (h % 2) * 128
                    wait("tensor", "s")
                    emit("tensor", lambda e, h=h, pb2=pb2: e.transpose(
                        ptrb[0:64, pb2:pb2 + 128], attn_sb[:, h * 64:(h + 1) * 64], identbf[:]),
                        incs=[("t", 1)])
                    wait("scalar", "t")
                    emit("scalar", lambda e, h=h, pb2=pb2: e.copy(
                        out=attnT[(h % 2) * 64:(h % 2) * 64 + 64, (h // 2) * 128:(h // 2) * 128 + 128],
                        in_=ptrb[0:64, pb2:pb2 + 128]), incs=[("s", 1)])
                wait("tensor", "s")
                for hf in range(2):
                    for c in range(8):
                        emit("tensor", lambda e, c=c, hf=hf: e.matmul(
                            pbig[:, hf * 512:hf * 512 + 512], attnT[:, c * 128:(c + 1) * 128],
                            wb["wo"][:, c * 1024 + hf * 512:c * 1024 + hf * 512 + 512],
                            start=(c == 0), stop=(c == 7)),
                            incs=([("t", 1)] if (c == 7 and hf == 1) else []))
                wait("vector", "t")
                wait("vector", "xal")
                emit("vector", lambda e, b=b: e.tensor_tensor(out=x1sb[:], in0=pbig[:, 0:1024],
                                                              in1=xa[b][:], op=ALU.add), incs=[("v", 1)])
                if t >= 1:
                    wait("vector", "x2st")
                ffn(x1sb, x2sb[b])
                wait("gpsimd", "v")
                emit("gpsimd", lambda e, t=t, b=b: e.dma_start(out=x2S[t], in_=x2sb[b][:]), incs=[("x2st", 16)])
            barrier()

            conv_weights(wconv_swa)
            swa_sc = [0]
            swa_fin = [0]
            ring_rd = {}
            for t in range(nt):
                b = t % 2
                slot = t % 3
                if t >= 1:
                    wait("sync", "v")
                emit("sync", lambda e, t=t, b=b: e.dma_start(out=x2in[b][:], in_=x2S[t]), incs=[("x2ld", 16)])
                wait("vector", "x2ld")
                layernorm(x2in[b])
                transposes_xn()
                for iw, (wname, dst, bias) in enumerate((("qw", qTs, qbr), ("kw", None, kbr))):
                    for h in range(NH):
                        pb2 = (h % 2) * 128
                        wait("tensor", "s")
                        for c in range(8):
                            emit("tensor", lambda e, h=h, c=c, pb2=pb2, wname=wname: e.matmul(
                                pq64[0:64, pb2:pb2 + 128],
                                wb[wname][:, c * 1024 + h * 64:c * 1024 + h * 64 + 64],
                                xnT[:, c * 128:(c + 1) * 128],
                                start=(c == 0), stop=(c == 7)), incs=([("t", 1)] if c == 7 else []))
                        wait("scalar", "t")
                        if iw == 0:
                            emit("scalar", lambda e, h=h, pb2=pb2: e.activation(
                                out=qTs[0:64, h * 128:(h + 1) * 128], in_=pq64[0:64, pb2:pb2 + 128],
                                func=AF.Identity, bias=qbr[0:64, h:h + 1]), incs=[("s", 1)])
                        else:
                            emit("scalar", lambda e, h=h, pb2=pb2, slot=slot: e.activation(
                                out=kring[0:64, slot * 2048 + h * 128:slot * 2048 + (h + 1) * 128],
                                in_=pq64[0:64, pb2:pb2 + 128],
                                func=AF.Identity, bias=kbr[0:64, h:h + 1]), incs=[("s", 1)])
                wait("tensor", "s")
                for hf in range(2):
                    for c in range(8):
                        emit("tensor", lambda e, c=c, hf=hf: e.matmul(
                            pbig[:, hf * 512:hf * 512 + 512], xnT[:, c * 128:(c + 1) * 128],
                            wb["vw"][:, c * 1024 + hf * 512:c * 1024 + hf * 512 + 512],
                            start=(c == 0), stop=False))
                    emit("tensor", lambda e, hf=hf: e.matmul(
                        pbig[:, hf * 512:hf * 512 + 512], onesbf[:], vbbf[:, hf * 512:hf * 512 + 512],
                        start=False, stop=True), incs=([("t", 1)] if hf == 1 else []))
                wait("vector", "t")
                emit("vector", lambda e, slot=slot: e.tensor_copy(
                    out=vring[:, slot * 1024:(slot + 1) * 1024], in_=pbig[:, 0:1024]),
                    incs=[("v", 1)])
                slots = list(range(max(0, t - 2), t + 1))
                for si, k in enumerate(slots):
                    ks = k % 3
                    wait("tensor", "v")
                    wait("tensor", "s")
                    if si == 0:
                        wait("tensor", "v", swa_fin[0])
                        wait("tensor", "s", swa_sc[0])
                    for h in range(NH):
                        emit("tensor", lambda e, h=h, ks=ks: e.matmul(
                            pbig[:, h * 128:(h + 1) * 128],
                            kring[0:64, ks * 2048 + h * 128:ks * 2048 + (h + 1) * 128],
                            qTs[0:64, h * 128:(h + 1) * 128],
                            start=(h % 4 == 0), stop=(h % 4 == 3)), incs=([("t", 1)] if h == NH - 1 else []))
                    pt2 = ptb[si % 2]
                    wait("scalar", "t")
                    wait("scalar", "v", swa_fin[0])
                    wait("scalar", "t", pvmark.get(("spt", si % 2), 0))
                    emit("scalar", lambda e, pt2=pt2: e.activation(out=pt2[:], in_=pbig[:, 0:2048],
                                                                   func=AF.Exp, scale=SCL_SWA),
                         incs=[("s", 1)])
                    swa_sc[0] = cnt["s"]
                    msk = mlow if k == t else (mhigh if k == t - 2 else None)
                    if msk is not None:
                        wait("vector", "s")
                        for h in range(NH):
                            emit("vector", lambda e, h=h, pt2=pt2, msk=msk: e.tensor_tensor(
                                out=pt2[:, h * 128:(h + 1) * 128], in0=pt2[:, h * 128:(h + 1) * 128],
                                in1=msk[:], op=ALU.mult), incs=([("v", 1)] if h == NH - 1 else []))
                        wait("tensor", "v")
                    else:
                        wait("tensor", "s")
                    for h in range(NH):
                        emit("tensor", lambda e, h=h, pt2=pt2, ks=ks, si=si, n=len(slots): e.matmul(
                            acc[:, h * 64:(h + 1) * 64], pt2[:, h * 128:(h + 1) * 128],
                            vring[:, (ks * NH + h) * 64:(ks * NH + h) * 64 + 64],
                            start=(si == 0 and h % 8 == 0), stop=(si == n - 1 and h % 8 == 7)))
                        emit("tensor", lambda e, h=h, pt2=pt2, si=si, n=len(slots): e.matmul(
                            pden[:, h:h + 1], pt2[:, h * 128:(h + 1) * 128],
                            onecbf[:],
                            start=(si == 0 and h == 0), stop=(si == n - 1 and h == NH - 1)),
                            incs=([("t", 1)] if h == NH - 1 else []))
                    pvmark[("spt", si % 2)] = cnt["t"]
                ring_rd[slot] = cnt["t"]
                wait("vector", "t")
                emit("vector", lambda e: e.reciprocal(out=rec_sb[:, 0:NH], in_=pden[:, 0:NH]),
                     incs=[("v", 1)])
                wait("vector", "v")
                for h in range(NH):
                    emit("vector", lambda e, h=h: e.tensor_scalar(
                        out=attn_sb[:, h * 64:(h + 1) * 64], in0=acc[:, h * 64:h * 64 + 64],
                        scalar1=rec_sb[:, h:h + 1], scalar2=None, op0=ALU.mult),
                        incs=([("v", 1)] if h == NH - 1 else []))
                swa_fin[0] = cnt["v"]
                wait("tensor", "v")
                for h in range(NH):
                    pb2 = (h % 2) * 128
                    wait("tensor", "s")
                    emit("tensor", lambda e, h=h, pb2=pb2: e.transpose(
                        ptrb[0:64, pb2:pb2 + 128], attn_sb[:, h * 64:(h + 1) * 64], identbf[:]),
                        incs=[("t", 1)])
                    wait("scalar", "t")
                    emit("scalar", lambda e, h=h, pb2=pb2: e.copy(
                        out=attnT[(h % 2) * 64:(h % 2) * 64 + 64, (h // 2) * 128:(h // 2) * 128 + 128],
                        in_=ptrb[0:64, pb2:pb2 + 128]), incs=[("s", 1)])
                wait("tensor", "s")
                for hf in range(2):
                    for c in range(8):
                        emit("tensor", lambda e, c=c, hf=hf: e.matmul(
                            pbig[:, hf * 512:hf * 512 + 512], attnT[:, c * 128:(c + 1) * 128],
                            wb["ow"][:, c * 1024 + hf * 512:c * 1024 + hf * 512 + 512],
                            start=(c == 0), stop=False))
                    emit("tensor", lambda e, hf=hf: e.matmul(
                        pbig[:, hf * 512:hf * 512 + 512], onesbf[:], obbf[:, hf * 512:hf * 512 + 512],
                        start=False, stop=True), incs=([("t", 1)] if hf == 1 else []))
                wait("vector", "t")
                emit("vector", lambda e, b=b: e.tensor_tensor(out=x3sb[:], in0=pbig[:, 0:1024],
                                                              in1=x2in[b][:], op=ALU.add), incs=[("v", 1)])
                if t >= 1:
                    wait("vector", "ost")
                ffn(x3sb, x4sb[b])
                wait("gpsimd", "v")
                emit("gpsimd", lambda e, t=t, b=b: e.dma_start(
                    out=out.rearrange("(n p) d -> n p d", p=128)[t], in_=x4sb[b][:]), incs=[("ost", 16)])
            wait("gpsimd", "ost", nt * 16)

        block = E(nc.Block())

        @block.sync
        def _(e):
            program("sync")

        @block.scalar
        def _(e):
            program("scalar")

        @block.tensor
        def _(e):
            program("tensor")

        @block.vector
        def _(e):
            program("vector")

        @block.gpsimd
        def _(e):
            program("gpsimd")

    return nc


def _lay(w):
    k, m = w.shape
    return np.ascontiguousarray(w.reshape(k // 128, 128, m).transpose(1, 0, 2).reshape(128, -1))


def host_inputs(inp, S):
    perm32 = np.concatenate([np.arange(0, 32, 2), np.arange(1, 32, 2)])
    wqr = np.asarray(inp["L0_attn_a_W_QR"], np.float32).copy()
    wqr = wqr.reshape(DL, NH, 32)[:, :, perm32].reshape(DL, NH * 32)
    wkr = np.asarray(inp["L0_attn_a_W_KR"], np.float32)[:, perm32]
    d = {
        "wdq": _lay(np.asarray(inp["L0_attn_a_W_DQ"], np.float32)),
        "wuq": _lay(np.asarray(inp["L0_attn_a_W_UQ"], np.float32)),
        "wqr": _lay(wqr),
        "wdkv": _lay(np.asarray(inp["L0_attn_a_W_DKV"], np.float32)),
        "wuk": _lay(np.asarray(inp["L0_attn_a_W_UK"], np.float32)),
        "wuv": _lay(np.asarray(inp["L0_attn_a_W_UV"], np.float32)),
        "wkr": _lay(wkr),
        "wo": _lay(np.asarray(inp["L0_attn_a_W_O"], np.float32)),
        "u1": _lay(np.asarray(inp["L0_ffn_U1"], np.float32)),
        "v1": _lay(np.asarray(inp["L0_ffn_V1"], np.float32)),
        "u2": _lay(np.asarray(inp["L0_ffn_U2"], np.float32)),
        "v2": _lay(np.asarray(inp["L0_ffn_V2"], np.float32)),
        "qw": _lay(np.ascontiguousarray(np.asarray(inp["L0_attn_b_q_w"], np.float32).T)),
        "kw": _lay(np.ascontiguousarray(np.asarray(inp["L0_attn_b_k_w"], np.float32).T)),
        "vw": _lay(np.ascontiguousarray(np.asarray(inp["L0_attn_b_v_w"], np.float32).T)),
        "ow": _lay(np.ascontiguousarray(np.asarray(inp["L0_attn_b_o_w"], np.float32).T)),
        "b1r": np.ascontiguousarray(np.asarray(inp["L0_ffn_b1"], np.float32).reshape(32, 128).T),
        "qbr": np.ascontiguousarray(np.asarray(inp["L0_attn_b_q_b"], np.float32).reshape(16, 64).T),
        "kbr": np.ascontiguousarray(np.asarray(inp["L0_attn_b_k_b"], np.float32).reshape(16, 64).T),
        "b2row": np.asarray(inp["L0_ffn_b2"], np.float32).reshape(1, D),
        "vbrow": np.asarray(inp["L0_attn_b_v_b"], np.float32).reshape(1, D),
        "obrow": np.asarray(inp["L0_attn_b_o_b"], np.float32).reshape(1, D),
    }
    pos = np.arange(S, dtype=np.float32)
    inv = (10000.0 ** (-np.arange(0, 32, 2, dtype=np.float32) / 32.0))
    ang = pos[:, None] * inv[None, :]
    d["ropec"] = np.ascontiguousarray(np.tile(np.cos(ang), (1, 32)).astype(np.float32))
    d["ropes"] = np.ascontiguousarray(np.tile(np.sin(ang), (1, 32)).astype(np.float32))
    ii = np.arange(128)
    d["mlow"] = (ii[:, None] <= ii[None, :]).astype(np.float32)
    d["mhigh"] = (ii[:, None] > ii[None, :]).astype(np.float32)
    d["identbf"] = np.eye(128, dtype=np.float32).astype(ml_dtypes.bfloat16)
    d["onescol"] = np.ones((1, 128), np.float32)
    d["ones128"] = np.ones((128, 1), np.float32)
    d["cst"] = np.tile(np.array(CVALS, np.float32), (128, 1))
    return d


def kernel(**inputs):
    S = 4096
    nt = S // 128
    shared = host_inputs(inputs, S)
    x = np.asarray(inputs["L0_attn_a_x"], np.float32)
    nc = build(nt)
    in_maps = []
    for i in range(8):
        m = dict(shared)
        m["x"] = np.ascontiguousarray(x[i])
        in_maps.append(m)
    res = run_bass_kernel_spmd(nc, in_maps, core_ids=list(range(8)))
    out = np.stack([res.results[i]["out"] for i in range(8)], axis=0)
    return out.astype(np.float32)


# version 04 (final)
# speedup vs baseline: 1.1380x; 1.0718x over previous
import math
import numpy as np
import ml_dtypes
from contextlib import ExitStack

import concourse.bass as bass
import concourse.mybir as mybir
from concourse.bass_utils import run_bass_kernel_spmd

F32 = mybir.dt.float32
BF16 = mybir.dt.bfloat16
AF = mybir.ActivationFunctionType
ALU = mybir.AluOpType
AX = mybir.AxisListType

D = 1024
NH = 16
DH = 64
DHR = 32
DL = 256
DFF = 4096
RANK = 256
WIN = 256
EPS = 1e-5
SCL_MLA = 1.0 / math.sqrt(DH + DHR)
SCL_SWA = 1.0 / math.sqrt(DH)
GC1 = 0.7978845608028654
GC2 = 0.7978845608028654 * 0.044715
CVALS = [EPS, -0.5, SCL_MLA, SCL_SWA, 80.0, -80.0, 1.0, -2.0, GC1, GC2, 0.5, 2.0]

ENGS = ("sync", "scalar", "tensor", "vector", "gpsimd")


def build(nt):
    S = nt * 128
    nc = bass.Bass("TRN2", target_bir_lowering=False, debug=False)
    DT = {}

    def din(name, shape, dt=F32):
        DT[name] = nc.dram_tensor(name, shape, dt, kind="ExternalInput").ap()
        return DT[name]

    din("x", [S, D])
    for w, sh in [("wdq", [128, 8 * 256]), ("wuq", [128, 2 * 1024]), ("wqr", [128, 2 * 512]),
                  ("wdkv", [128, 8 * 256]), ("wuk", [128, 2 * 1024]), ("wuv", [128, 2 * 1024]),
                  ("wkr", [128, 8 * 32]), ("wo", [128, 8 * 1024]),
                  ("u1", [128, 8 * 256]), ("v1", [128, 2 * 4096]), ("u2", [128, 32 * 256]),
                  ("v2", [128, 2 * 1024]),
                  ("qw", [128, 8 * 1024]), ("kw", [128, 8 * 1024]), ("vw", [128, 8 * 1024]),
                  ("ow", [128, 8 * 1024])]:
        din(w, sh)
    din("b1r", [128, 32])
    din("qbr", [64, 16])
    din("kbr", [64, 16])
    din("b2row", [1, D])
    din("vbrow", [1, D])
    din("obrow", [1, D])
    din("ropec", [S, 512])
    din("ropes", [S, 512])
    din("mlow", [128, 128])
    din("mhigh", [128, 128])
    din("identbf", [128, 128], BF16)
    din("onescol", [1, 128])
    din("ones128", [128, 1])
    din("cst", [128, len(CVALS)])
    out = nc.dram_tensor("out", [S, D], F32, kind="ExternalOutput").ap()
    kcS = nc.dram_tensor("kcS", [nt, 64, 2048], BF16, kind="Internal").ap()
    krS = nc.dram_tensor("krS", [nt, 32, 2048], BF16, kind="Internal").ap()
    qcS = nc.dram_tensor("qcS", [nt, 64, 2048], BF16, kind="Internal").ap()
    qrS = nc.dram_tensor("qrS", [nt, 32, 2048], BF16, kind="Internal").ap()
    vS = nc.dram_tensor("vS", [nt, 128, NH * 64], BF16, kind="Internal").ap()
    x2S = nc.dram_tensor("x2S", [nt, 128, D], F32, kind="Internal").ap()

    with ExitStack() as stk:
        E = stk.enter_context
        sb = {}

        def T(name, shape, dt=F32):
            sb[name] = E(nc.sbuf_tensor(name, shape, dt))
            return sb[name]

        arena = T("arena", [128, 32768], BF16)

        class ASlice:
            def __init__(self, p0, p1, c0, cols, base=None):
                self.base = arena if base is None else base
                self.p0, self.p1, self.c0, self.cols = p0, p1, c0, cols
                self.shape = (p1 - p0, cols)

            def __getitem__(self, idx):
                ps, cs = idx
                a0 = self.p0 if ps.start is None else self.p0 + ps.start
                a1 = self.p1 if ps.stop is None else self.p0 + ps.stop
                b0 = self.c0 if cs.start is None else self.c0 + cs.start
                b1 = self.c0 + self.cols if cs.stop is None else self.c0 + cs.stop
                return self.base[a0:a1, b0:b1]

        wb = {}
        _off = [0]

        def AV(cols, p0=0, p1=128, at=None):
            c0 = _off[0] if at is None else at
            if at is None:
                _off[0] += cols
            return ASlice(p0, p1, c0, cols)

        for w, cols in [("wdq", 8 * 256), ("wuq", 2 * 1024), ("wqr", 2 * 512), ("wdkv", 8 * 256),
                        ("wuk", 2 * 1024), ("wuv", 2 * 1024), ("wkr", 8 * 32)]:
            wb[w] = AV(cols)
        kcA = AV(2048, 0, 64)
        krA = ASlice(64, 96, kcA.c0, 2048)
        qcA = AV(2048, 0, 64)
        qrA = ASlice(64, 96, qcA.c0, 2048)
        vA = AV(NH * 64)
        wb["wo"] = ASlice(0, 128, 0, 8 * 1024)
        kbuf = [ASlice(0, 96, 8192, 2048), ASlice(0, 96, 10240, 2048)]
        qbuf = ASlice(0, 96, 12288, 2048)
        vbuf = [ASlice(0, 128, 14336, NH * 64), ASlice(0, 128, 14336 + NH * 64, NH * 64)]
        for i, w in enumerate(("qw", "kw", "vw", "ow")):
            wb[w] = ASlice(0, 128, i * 8192, 8 * 1024)
        for w, cols in [("u1", 8 * 256), ("v1", 2 * 4096), ("u2", 32 * 256), ("v2", 2 * 1024)]:
            wb[w] = T("W_" + w, [128, cols], BF16)
        b1r = T("sb_b1r", [128, 32])
        qbr = T("sb_qbr", [64, 16])
        kbr = T("sb_kbr", [64, 16])
        browz = T("browz", [1, D])
        b2bf = T("b2bf", [1, D], BF16)
        vbbf = T("vbbf", [1, D], BF16)
        obbf = T("obbf", [1, D], BF16)
        onesbf = T("onesbf", [1, 128], BF16)
        onecbf = T("onecbf", [128, 1], BF16)
        mlow = T("sb_mlow", [128, 128])
        mhigh = T("sb_mhigh", [128, 128])
        identbf = T("sb_identbf", [128, 128], BF16)
        onescol = T("sb_onescol", [1, 128])
        cstb = T("cstb", [128, len(CVALS)])
        for i, v in enumerate(CVALS):
            nc.const_aps.register(F32, v, cstb[:, i:i + 1])
        stg = [T("stg0", [128, 2048])]

        xa = [T("xa0", [128, D]), T("xa1", [128, D])]
        st12 = T("st12", [128, 12])
        mv = T("mv", [128, 2])
        lnv = T("lnv", [128, 1])
        rstd = T("rstd", [128, 1])
        xnbf = T("xnbf", [128, D], BF16)
        xnT = T("xnT", [128, 8 * 128], BF16)

        qrbf = T("qrbf", [128, 512], BF16)
        krbf = T("krbf", [128, 32], BF16)
        krT = T("krT", [32, 128], BF16)
        ckv = T("ckv", [128, 2 * 128], BF16)
        cq = T("cq", [128, 2 * 128], BF16)
        ptb = [T("ptb0", [128, 2048], BF16), T("ptb1", [128, 2048], BF16)]
        attn_sb = T("attn_sb", [128, NH * 64], BF16)
        rec_sb = T("rec_sb", [128, NH])
        attnT = T("attnT", [128, 8 * 128], BF16)
        x1sb = T("x1sb", [128, D])
        x2sb = [T("x2sb0", [128, D]), T("x2sb1", [128, D])]
        ptf = T("ptf", [128, 256], BF16)
        hbb = [T("hb0", [128, 512]), T("hb1", [128, 512])]
        bufA = T("bufA", [128, 512])
        bufB = T("bufB", [128, 512])
        bufC = T("bufC", [128, 512])
        bufD = T("bufD", [128, 512])
        sq, u0, uu, ee, dd, sg = bufA, bufB, bufA, bufB, bufB, bufA
        ropeA, ropeB = bufA, bufB
        rc = [bufC]
        rs = [bufD]
        htb = T("htb", [128, 32 * 128], BF16)
        q2t = T("q2t", [128, 256], BF16)

        qTs = T("qTs", [64, NH * 128], BF16)
        kring = T("kring", [64, 3 * NH * 128], BF16)
        vring = T("vring", [128, 3 * NH * 64], BF16)
        x3sb = x1sb
        x4sb = x2sb
        x2in = xa

        pbig = E(nc.psum_tensor("pbig", [128, 2048], F32))
        acc = E(nc.psum_tensor("acc", [128, NH * 64], F32))
        pbf = E(nc.psum_tensor("pbf", [128, 512], BF16))
        pf32 = E(nc.psum_tensor("pf32", [128, 336], F32))
        ptrb = ASlice(0, 64, 0, 256, base=pbf)
        ptrb2 = ASlice(0, 128, 256, 256, base=pbf)
        pq64 = ASlice(0, 64, 0, 256, base=pf32)
        pmisc = ASlice(0, 128, 256, 64, base=pf32)
        pden = ASlice(0, 128, 320, 16, base=pf32)

        sems = {}
        for sname in ("wld", "cv", "xal", "rpl", "t", "s", "v", "gw", "kld0", "kld1", "qld",
                      "xld", "x2st", "x2ld", "ost", "ph"):
            sems[sname] = E(nc.semaphore(sname))

        DT_ones128_f32 = T("ones128f", [128, 1])
        DT_ones128 = DT_ones128_f32[:]

        def program(active):
            eng = getattr(nc, active)
            cnt = {k: 0 for k in sems}

            def emit(engine, f, incs=()):
                r = f(getattr(nc, engine)) if engine == active else None
                for sname, amt in incs:
                    cnt[sname] += amt
                    if r is not None:
                        r.then_inc(sems[sname], amt)
                return cnt

            def wait(engine, sname, val=None):
                v = cnt[sname] if val is None else val
                if v <= 0:
                    return
                if engine == active:
                    eng.wait_ge(sems[sname], v)

            def barrier():
                for sn in ("wld", "xal", "rpl", "kld0", "kld1", "qld", "x2ld"):
                    wait("sync", sn)
                for sn in ("gw", "x2st", "ost"):
                    wait("gpsimd", sn)
                for e in ENGS:
                    if e == active:
                        eng.sem_inc(sems["ph"], 1)
                    cnt["ph"] += 1
                tgt = cnt["ph"]
                for e in ENGS:
                    wait(e, "ph", tgt)

            smalls = [("mlow", mlow), ("mhigh", mhigh), ("identbf", identbf),
                      ("onescol", onescol), ("ones128", DT_ones128_f32), ("cst", cstb),
                      ("b1r", b1r), ("qbr", qbr), ("kbr", kbr)]
            for name, dst in smalls:
                emit("sync", lambda e, n=name, d=dst: e.dma_start(out=d[:], in_=DT[n][:]),
                     incs=[("wld", 16)])
            wait("scalar", "wld")
            emit("scalar", lambda e: e.copy(out=onesbf[:], in_=onescol[:]), incs=[("cv", 1)])
            emit("scalar", lambda e: e.copy(out=onecbf[:], in_=DT_ones128_f32[:]), incs=[("cv", 1)])
            for name, dst in (("b2row", b2bf), ("vbrow", vbbf), ("obrow", obbf)):
                wait("sync", "cv")
                emit("sync", lambda e, n=name: e.dma_start(out=browz[:], in_=DT[n][:]), incs=[("wld", 16)])
                wait("scalar", "wld")
                emit("scalar", lambda e, d=dst: e.copy(out=d[:], in_=browz[:]), incs=[("cv", 1)])
            wconv = [(w, wb[w].shape[1]) for w in ("wdq", "wuq", "wqr", "wdkv", "wuk", "wuv",
                                                   "wkr", "u1", "v1", "u2", "v2")]
            wconv_swa = [(w, 8192) for w in ("qw", "kw", "vw", "ow")]

            def conv_weights(lst):
                si = [0]
                for w, cols in lst:
                    for c0 in range(0, cols, 2048):
                        cw = min(2048, cols - c0)
                        wait("sync", "cv")
                        emit("sync", lambda e, w=w, c0=c0, cw=cw: e.dma_start(
                            out=stg[0][:, 0:cw], in_=DT[w][:, c0:c0 + cw]), incs=[("wld", 16)])
                        wait("scalar", "wld")
                        emit("scalar", lambda e, w=w, c0=c0, cw=cw: e.copy(
                            out=wb[w][:, c0:c0 + cw], in_=stg[0][:, 0:cw]), incs=[("cv", 1)])
                        si[0] += 1

            conv_weights(wconv)
            barrier()

            def layernorm(xin, war_t=None):
                wait("vector", "t", war_t)
                wait("vector", "v")
                emit("vector", lambda e: e.bn_stats(out=st12[:, 0:6], in_=xin[:, 0:512]), incs=[("v", 1)])
                emit("vector", lambda e: e.bn_stats(out=st12[:, 6:12], in_=xin[:, 512:1024]), incs=[("v", 1)])
                wait("vector", "v")
                emit("vector", lambda e: e.bn_aggr(out=mv[:], in_=st12[:]), incs=[("v", 1)])
                wait("scalar", "v")
                emit("scalar", lambda e: e.activation(out=lnv[:], in_=mv[:, 1:2], func=AF.Ln, bias=EPS),
                     incs=[("s", 1)])
                wait("scalar", "s")
                emit("scalar", lambda e: e.activation(out=rstd[:], in_=lnv[:], func=AF.Exp, scale=-0.5),
                     incs=[("s", 1)])
                wait("vector", "s")
                emit("vector", lambda e: e.tensor_scalar(out=xnbf[:], in0=xin[:], scalar1=mv[:, 0:1],
                                                         scalar2=rstd[:], op0=ALU.subtract, op1=ALU.mult),
                     incs=[("v", 1)])

            def transposes_xn():
                for c in range(8):
                    b = c % 2
                    wait("tensor", "v")
                    wait("tensor", "s")
                    emit("tensor", lambda e, c=c, b=b: e.transpose(
                        ptrb2[:, b * 128:(b + 1) * 128], xnbf[:, c * 128:(c + 1) * 128], identbf[:]),
                        incs=[("t", 1)])
                    wait("scalar", "t")
                    emit("scalar", lambda e, c=c, b=b: e.copy(
                        out=xnT[:, c * 128:(c + 1) * 128], in_=ptrb2[:, b * 128:(b + 1) * 128]),
                        incs=[("s", 1)])

            def ffn(xin, xout):
                layernorm(xin)
                transposes_xn()
                wait("tensor", "s")
                for l in range(2):
                    for c in range(8):
                        emit("tensor", lambda e, l=l, c=c: e.matmul(
                            pbig[:, l * 128:(l + 1) * 128],
                            wb["u1"][:, c * 256 + l * 128:c * 256 + (l + 1) * 128],
                            xnT[:, c * 128:(c + 1) * 128],
                            start=(l == 0 and c == 0), stop=(l == 1 and c == 7)),
                            incs=([("t", 1)] if c == 7 else []))
                wait("scalar", "t")
                emit("scalar", lambda e: e.copy(out=ptf[:], in_=pbig[:, 0:256]), incs=[("s", 1)])
                idmark = {0: 0, 1: 0}
                htmark = {0: 0, 1: 0}
                for g in range(8):
                    pb = 1024 + (g % 2) * 512
                    hb = hbb[g % 2]
                    A_ = bufA if g % 2 == 0 else bufC
                    B_ = bufB if g % 2 == 0 else bufD
                    wait("tensor", "s", idmark[g % 2])
                    if g == 0:
                        wait("tensor", "s")
                    for jj in range(4):
                        j = g * 4 + jj
                        for l in range(2):
                            emit("tensor", lambda e, j=j, l=l, pb=pb, jj=jj: e.matmul(
                                pbig[:, pb + jj * 128:pb + (jj + 1) * 128],
                                wb["v1"][:, l * 4096 + j * 128:l * 4096 + (j + 1) * 128],
                                ptf[:, l * 128:(l + 1) * 128],
                                start=(jj == 0 and l == 0), stop=(jj == 3 and l == 1)),
                                incs=([("t", 1)] if (jj == 3 and l == 1) else []))
                    wait("scalar", "t")
                    wait("scalar", "v", htmark[g % 2])
                    for jj in range(4):
                        j = g * 4 + jj
                        emit("scalar", lambda e, j=j, jj=jj, pb=pb, hb=hb: e.activation(
                            out=hb[:, jj * 128:(jj + 1) * 128], in_=pbig[:, pb + jj * 128:pb + (jj + 1) * 128],
                            func=AF.Identity, bias=b1r[:, j:j + 1]), incs=[("s", 1)])
                    idmark[g % 2] = cnt["s"]
                    wait("scalar", "s")
                    wait("scalar", "v", htmark[g % 2])
                    emit("scalar", lambda e, hb=hb, A_=A_: e.activation(out=A_[:], in_=hb[:], func=AF.Square),
                         incs=[("s", 1)])
                    wait("vector", "s")
                    emit("vector", lambda e, A_=A_, B_=B_: e.tensor_scalar(out=B_[:], in0=A_[:], scalar1=GC2,
                                                                           scalar2=GC1, op0=ALU.mult, op1=ALU.add),
                         incs=[("v", 1)])
                    wait("vector", "v")
                    emit("vector", lambda e, hb=hb, A_=A_, B_=B_: e.tensor_tensor(out=A_[:], in0=B_[:], in1=hb[:], op=ALU.mult),
                         incs=[("v", 1)])
                    wait("scalar", "v")
                    emit("scalar", lambda e, A_=A_, B_=B_: e.activation(out=B_[:], in_=A_[:], func=AF.Exp, scale=-2.0),
                         incs=[("s", 1)])
                    wait("vector", "s")
                    emit("vector", lambda e, B_=B_: e.tensor_scalar_add(out=B_[:], in0=B_[:], scalar1=1.0),
                         incs=[("v", 1)])
                    wait("vector", "v")
                    emit("vector", lambda e, A_=A_, B_=B_: e.reciprocal(out=A_[:], in_=B_[:]), incs=[("v", 1)])
                    wait("vector", "v")
                    emit("vector", lambda e, g=g, hb=hb, A_=A_: e.tensor_tensor(out=htb[:, g * 512:(g + 1) * 512],
                                                                                in0=hb[:], in1=A_[:], op=ALU.mult),
                         incs=[("v", 1)])
                    htmark[g % 2] = cnt["v"]
                wait("tensor", "v")
                for l in range(2):
                    for j in range(32):
                        emit("tensor", lambda e, l=l, j=j: e.matmul(
                            pbig[:, 256 + l * 128:256 + (l + 1) * 128],
                            wb["u2"][:, j * 256 + l * 128:j * 256 + (l + 1) * 128],
                            htb[:, j * 128:(j + 1) * 128],
                            start=(l == 0 and j == 0), stop=(l == 1 and j == 31)),
                            incs=([("t", 1)] if j == 31 else []))
                wait("scalar", "t")
                emit("scalar", lambda e: e.copy(out=q2t[:], in_=pbig[:, 256:512]), incs=[("s", 1)])
                wait("tensor", "s")
                for hf in range(2):
                    for l in range(2):
                        emit("tensor", lambda e, l=l, hf=hf: e.matmul(
                            pbig[:, 1024 + hf * 512:1536 + hf * 512], q2t[:, l * 128:(l + 1) * 128],
                            wb["v2"][:, l * 1024 + hf * 512:l * 1024 + hf * 512 + 512],
                            start=(l == 0), stop=False))
                    emit("tensor", lambda e, hf=hf: e.matmul(
                        pbig[:, 1024 + hf * 512:1536 + hf * 512], onesbf[:], b2bf[:, hf * 512:hf * 512 + 512],
                        start=False, stop=True), incs=([("t", 1)] if hf == 1 else []))
                wait("vector", "t")
                emit("vector", lambda e, xin=xin, xout=xout: e.tensor_tensor(
                    out=xout[:], in0=pbig[:, 1024:2048], in1=xin[:], op=ALU.add), incs=[("v", 1)])

            xv = DT["x"].rearrange("(n p) d -> n p d", p=128)
            rcv = DT["ropec"].rearrange("(n p) d -> n p d", p=128)
            rsv = DT["ropes"].rearrange("(n p) d -> n p d", p=128)
            for t in range(nt):
                b = t % 2
                if t >= 1:
                    wait("sync", "v")
                emit("sync", lambda e, t=t, b=b: e.dma_start(out=xa[b][:], in_=xv[t]), incs=[("xal", 16)])
                emit("sync", lambda e, t=t: e.dma_start(out=rc[0][:], in_=rcv[t]), incs=[("rpl", 16)])
                emit("sync", lambda e, t=t: e.dma_start(out=rs[0][:], in_=rsv[t]), incs=[("rpl", 16)])
                wait("vector", "xal")
                layernorm(xa[b])
                transposes_xn()
                wait("tensor", "s")
                for i, (wname, dst) in enumerate((("wdkv", ckv), ("wdq", cq))):
                    for l in range(2):
                        for c in range(8):
                            emit("tensor", lambda e, i=i, l=l, c=c, wname=wname: e.matmul(
                                pbig[:, i * 256 + l * 128:i * 256 + (l + 1) * 128],
                                wb[wname][:, c * 256 + l * 128:c * 256 + (l + 1) * 128],
                                xnT[:, c * 128:(c + 1) * 128],
                                start=(i == 0 and l == 0 and c == 0),
                                stop=(i == 1 and l == 1 and c == 7)),
                                incs=([("t", 1)] if c == 7 else []))
                wait("scalar", "t")
                for i, dst in ((0, ckv), (1, cq)):
                    emit("scalar", lambda e, i=i, dst=dst: e.copy(out=dst[:], in_=pbig[:, i * 256:i * 256 + 256]),
                         incs=[("s", 1)])
                if t >= 1:
                    wait("scalar", "gw", t * 80)
                    wait("vector", "gw", t * 80)
                for i, (wname, csrc, dstA) in enumerate((("wuk", ckv, kcA), ("wuq", cq, qcA))):
                    for p in range(8):
                        pb = 512 + i * 256 + (p % 2) * 128
                        wait("tensor", "s")
                        for l in range(2):
                            emit("tensor", lambda e, p=p, l=l, pb=pb, wname=wname, csrc=csrc: e.matmul(
                                pbig[:, pb:pb + 128],
                                wb[wname][:, l * 1024 + p * 128:l * 1024 + (p + 1) * 128],
                                csrc[:, l * 128:(l + 1) * 128],
                                start=(l == 0), stop=(l == 1)), incs=([("t", 1)] if l == 1 else []))
                        wait("scalar", "t")
                        emit("scalar", lambda e, p=p, pb=pb, dstA=dstA: e.copy(
                            out=dstA[0:64, (2 * p) * 128:(2 * p) * 128 + 128], in_=pbig[0:64, pb:pb + 128]),
                            incs=[("s", 1)])
                        emit("scalar", lambda e, p=p, pb=pb, dstA=dstA: e.copy(
                            out=dstA[0:64, (2 * p + 1) * 128:(2 * p + 1) * 128 + 128],
                            in_=pbig[64:128, pb:pb + 128]), incs=[("s", 1)])
                wait("tensor", "s")
                for half in range(2):
                    for l in range(2):
                        emit("tensor", lambda e, l=l, hf=half: e.matmul(
                            pbig[:, 1024 + hf * 512:1536 + hf * 512], ckv[:, l * 128:(l + 1) * 128],
                            wb["wuv"][:, l * 1024 + hf * 512:l * 1024 + hf * 512 + 512],
                            start=(l == 0), stop=(l == 1)),
                            incs=([("t", 1)] if (l == 1 and half == 1) else []))
                wait("scalar", "t")
                emit("scalar", lambda e: e.copy(out=vA[:, 0:NH * 64], in_=pbig[:, 1024:2048]),
                     incs=[("s", 1)])
                wait("tensor", "s")
                for l in range(2):
                    emit("tensor", lambda e, l=l: e.matmul(
                        pbig[:, 0:512], cq[:, l * 128:(l + 1) * 128],
                        wb["wqr"][:, l * 512:(l + 1) * 512],
                        start=(l == 0), stop=(l == 1)), incs=([("t", 1)] if l == 1 else []))
                for c in range(8):
                    emit("tensor", lambda e, c=c: e.matmul(
                        pmisc[:, 0:32], xnT[:, c * 128:(c + 1) * 128],
                        wb["wkr"][:, c * 32:(c + 1) * 32],
                        start=(c == 0), stop=(c == 7)), incs=([("t", 1)] if c == 7 else []))
                wait("vector", "t")
                wait("vector", "rpl")
                emit("vector", lambda e: e.tensor_tensor(out=ropeA[:], in0=pbig[:, 0:512], in1=rc[0][:],
                                                              op=ALU.mult), incs=[("v", 1)])
                emit("vector", lambda e: e.tensor_tensor(out=ropeB[:], in0=pbig[:, 0:512], in1=rs[0][:],
                                                              op=ALU.mult), incs=[("v", 1)])
                wait("vector", "v")
                r3 = lambda ap: ap.rearrange("p (h two f) -> p h two f", two=2, f=16)
                emit("vector", lambda e: e.tensor_tensor(
                    out=r3(qrbf[:])[:, :, 0, :], in0=r3(ropeA[:])[:, :, 0, :],
                    in1=r3(ropeB[:])[:, :, 1, :], op=ALU.subtract), incs=[("v", 1)])
                emit("vector", lambda e: e.tensor_tensor(
                    out=r3(qrbf[:])[:, :, 1, :], in0=r3(ropeB[:])[:, :, 0, :],
                    in1=r3(ropeA[:])[:, :, 1, :], op=ALU.add), incs=[("v", 1)])
                wait("vector", "v")
                emit("vector", lambda e: e.tensor_tensor(out=ropeA[:, 0:32], in0=pmisc[:, 0:32],
                                                              in1=rc[0][:, 0:32], op=ALU.mult), incs=[("v", 1)])
                emit("vector", lambda e: e.tensor_tensor(out=ropeB[:, 0:32], in0=pmisc[:, 0:32],
                                                              in1=rs[0][:, 0:32], op=ALU.mult), incs=[("v", 1)])
                wait("vector", "v")
                emit("vector", lambda e: e.tensor_tensor(out=krbf[:, 0:16], in0=ropeA[:, 0:16],
                                                         in1=ropeB[:, 16:32], op=ALU.subtract), incs=[("v", 1)])
                emit("vector", lambda e: e.tensor_tensor(out=krbf[:, 16:32], in0=ropeB[:, 0:16],
                                                         in1=ropeA[:, 16:32], op=ALU.add), incs=[("v", 1)])
                wait("tensor", "v")
                for h in range(NH):
                    pb2 = (h % 2) * 128
                    wait("tensor", "s")
                    emit("tensor", lambda e, h=h, pb2=pb2: e.transpose(
                        ptrb[0:32, pb2:pb2 + 128], qrbf[:, h * 32:(h + 1) * 32], identbf[:]),
                        incs=[("t", 1)])
                    wait("scalar", "t")
                    emit("scalar", lambda e, h=h, pb2=pb2: e.copy(
                        out=qrA[0:32, h * 128:(h + 1) * 128], in_=ptrb[0:32, pb2:pb2 + 128]),
                        incs=[("s", 1)])
                wait("tensor", "s")
                emit("tensor", lambda e: e.transpose(ptrb[0:32, 0:128], krbf[:], identbf[:]), incs=[("t", 1)])
                wait("scalar", "t")
                emit("scalar", lambda e: e.copy(out=krT[:], in_=ptrb[0:32, 0:128]), incs=[("s", 1)])
                wait("vector", "s")
                for h in range(NH):
                    emit("vector", lambda e, h=h: e.tensor_copy(out=krA[0:32, h * 128:(h + 1) * 128],
                                                                in_=krT[:]),
                         incs=([("v", 1)] if h == NH - 1 else []))
                wait("gpsimd", "s")
                wait("gpsimd", "v")
                emit("gpsimd", lambda e, t=t: e.dma_start(out=kcS[t], in_=kcA[0:64, :]), incs=[("gw", 16)])
                emit("gpsimd", lambda e, t=t: e.dma_start(out=krS[t], in_=krA[0:32, :]), incs=[("gw", 16)])
                emit("gpsimd", lambda e, t=t: e.dma_start(out=qcS[t], in_=qcA[0:64, :]), incs=[("gw", 16)])
                emit("gpsimd", lambda e, t=t: e.dma_start(out=qrS[t], in_=qrA[0:32, :]), incs=[("gw", 16)])
                emit("gpsimd", lambda e, t=t: e.dma_start(out=vS[t], in_=vA[:, 0:NH * 64]), incs=[("gw", 16)])
            barrier()

            conv_weights([("wo", 8192)])
            g = [0]
            pvmark = {}
            scmark = {0: 0, 1: 0}
            finmark = [0]
            kvld = {}
            for t in range(nt):
                b = t % 2
                if t >= 1:
                    wait("sync", "v")
                emit("sync", lambda e, t=t: e.dma_start(out=qbuf[0:64, :], in_=qcS[t]), incs=[("qld", 16)])
                emit("sync", lambda e, t=t: e.dma_start(out=qbuf[64:96, :], in_=qrS[t]), incs=[("qld", 16)])
                emit("sync", lambda e, t=t, b=b: e.dma_start(out=xa[b][:], in_=xv[t]), incs=[("xal", 16)])
                kb0 = g[0] % 2
                sld = "kld%d" % kb0
                wait("sync", "t", pvmark.get(("k", kb0), 0))
                emit("sync", lambda e, kb0=kb0: e.dma_start(out=kbuf[kb0][0:64, :], in_=kcS[0]),
                     incs=[(sld, 16)])
                emit("sync", lambda e, kb0=kb0: e.dma_start(out=kbuf[kb0][64:96, :], in_=krS[0]),
                     incs=[(sld, 16)])
                emit("sync", lambda e, kb0=kb0: e.dma_start(out=vbuf[kb0][:, :], in_=vS[0]),
                     incs=[(sld, 16)])
                kvld[(t, 0)] = (sld, cnt[sld])
                for k in range(t + 1):
                    gk = g[0] + k
                    kb = gk % 2
                    if k + 1 <= t:
                        nb = (gk + 1) % 2
                        snb = "kld%d" % nb
                        wait("sync", "t", pvmark.get(("k", nb), 0))
                        emit("sync", lambda e, k=k, nb=nb: e.dma_start(out=kbuf[nb][0:64, :], in_=kcS[k + 1]),
                             incs=[(snb, 16)])
                        emit("sync", lambda e, k=k, nb=nb: e.dma_start(out=kbuf[nb][64:96, :], in_=krS[k + 1]),
                             incs=[(snb, 16)])
                        emit("sync", lambda e, k=k, nb=nb: e.dma_start(out=vbuf[nb][:, :], in_=vS[k + 1]),
                             incs=[(snb, 16)])
                        kvld[(t, k + 1)] = (snb, cnt[snb])
                    wait("tensor", kvld[(t, k)][0], kvld[(t, k)][1])
                    wait("tensor", "qld")
                    if k == 0:
                        wait("tensor", "v")
                    expm = {}
                    for grp in range(2):
                        h0 = grp * 8
                        wait("tensor", "s", scmark[grp] if isinstance(scmark, dict) else scmark[0])
                        for hh in range(8):
                            h = h0 + hh
                            emit("tensor", lambda e, h=h, kb=kb: e.matmul(
                                pbig[:, h * 128:(h + 1) * 128],
                                kbuf[kb][0:96, h * 128:(h + 1) * 128],
                                qbuf[0:96, h * 128:(h + 1) * 128],
                                start=(h % 4 == 0), stop=(h % 4 == 3)),
                                incs=([("t", 1)] if hh == 7 else []))
                        scm = cnt["t"]
                        wait("vector", "t", scm)
                        wait("vector", "s", scmark[grp] if isinstance(scmark, dict) else scmark[0])
                        emit("vector", lambda e, h0=h0: e.tensor_scalar(
                            out=pbig[:, h0 * 128:(h0 + 8) * 128], in0=pbig[:, h0 * 128:(h0 + 8) * 128],
                            scalar1=80.0, scalar2=-80.0, op0=ALU.min, op1=ALU.max), incs=[("v", 1)])
                        clm = cnt["v"]
                        wait("scalar", "v", clm)
                        wait("scalar", "t", pvmark.get(("pt", kb, grp), 0))
                        emit("scalar", lambda e, kb=kb, h0=h0: e.activation(
                            out=ptb[kb][:, h0 * 128:(h0 + 8) * 128], in_=pbig[:, h0 * 128:(h0 + 8) * 128],
                            func=AF.Exp, scale=SCL_MLA), incs=[("s", 1)])
                        scmark[grp] = cnt["s"]
                        expm[grp] = cnt["s"]
                        if k == t:
                            wait("vector", "s", expm[grp])
                            for hh in range(8):
                                h = h0 + hh
                                emit("vector", lambda e, h=h, kb=kb: e.tensor_tensor(
                                    out=ptb[kb][:, h * 128:(h + 1) * 128],
                                    in0=ptb[kb][:, h * 128:(h + 1) * 128],
                                    in1=mlow[:], op=ALU.mult), incs=([("v", 1)] if hh == 7 else []))
                            expm[(grp, "v")] = cnt["v"]
                    for grp in range(2):
                        h0 = grp * 8
                        if k == t:
                            wait("tensor", "v", expm[(grp, "v")])
                        else:
                            wait("tensor", "s", expm[grp])
                        for hh in range(8):
                            h = h0 + hh
                            emit("tensor", lambda e, h=h, kb=kb, k=k, t=t: e.matmul(
                                acc[:, h * 64:(h + 1) * 64], ptb[kb][:, h * 128:(h + 1) * 128],
                                vbuf[kb][:, h * 64:h * 64 + 64],
                                start=(k == 0 and h % 8 == 0), stop=(k == t and h % 8 == 7)))
                            emit("tensor", lambda e, h=h, kb=kb, k=k, t=t: e.matmul(
                                pden[:, h:h + 1], ptb[kb][:, h * 128:(h + 1) * 128],
                                onecbf[:],
                                start=(k == 0 and h == 0), stop=(k == t and h == NH - 1)),
                                incs=([("t", 1)] if (hh == 7) else []))
                        pvmark[("pt", kb, grp)] = cnt["t"]
                    pvmark[("k", kb)] = cnt["t"]
                g[0] += t + 1
                wait("vector", "t")
                emit("vector", lambda e: e.reciprocal(out=rec_sb[:, 0:NH], in_=pden[:, 0:NH]),
                     incs=[("v", 1)])
                wait("vector", "v")
                for h in range(NH):
                    emit("vector", lambda e, h=h: e.tensor_scalar(
                        out=attn_sb[:, h * 64:(h + 1) * 64], in0=acc[:, h * 64:h * 64 + 64],
                        scalar1=rec_sb[:, h:h + 1], scalar2=None, op0=ALU.mult),
                        incs=([("v", 1)] if h == NH - 1 else []))
                finmark[0] = cnt["v"]
                wait("tensor", "v")
                for h in range(NH):
                    pb2 = (h % 2) * 128
                    wait("tensor", "s")
                    emit("tensor", lambda e, h=h, pb2=pb2: e.transpose(
                        ptrb[0:64, pb2:pb2 + 128], attn_sb[:, h * 64:(h + 1) * 64], identbf[:]),
                        incs=[("t", 1)])
                    wait("scalar", "t")
                    emit("scalar", lambda e, h=h, pb2=pb2: e.copy(
                        out=attnT[(h % 2) * 64:(h % 2) * 64 + 64, (h // 2) * 128:(h // 2) * 128 + 128],
                        in_=ptrb[0:64, pb2:pb2 + 128]), incs=[("s", 1)])
                wait("tensor", "s")
                for hf in range(2):
                    for c in range(8):
                        emit("tensor", lambda e, c=c, hf=hf: e.matmul(
                            pbig[:, hf * 512:hf * 512 + 512], attnT[:, c * 128:(c + 1) * 128],
                            wb["wo"][:, c * 1024 + hf * 512:c * 1024 + hf * 512 + 512],
                            start=(c == 0), stop=(c == 7)),
                            incs=([("t", 1)] if (c == 7 and hf == 1) else []))
                wait("vector", "t")
                wait("vector", "xal")
                emit("vector", lambda e, b=b: e.tensor_tensor(out=x1sb[:], in0=pbig[:, 0:1024],
                                                              in1=xa[b][:], op=ALU.add), incs=[("v", 1)])
                if t >= 1:
                    wait("vector", "x2st")
                ffn(x1sb, x2sb[b])
                wait("gpsimd", "v")
                emit("gpsimd", lambda e, t=t, b=b: e.dma_start(out=x2S[t], in_=x2sb[b][:]), incs=[("x2st", 16)])
            barrier()

            conv_weights(wconv_swa)
            swa_sc = [0]
            swa_fin = [0]
            ring_rd = {}
            for t in range(nt):
                b = t % 2
                slot = t % 3
                if t >= 1:
                    wait("sync", "v")
                emit("sync", lambda e, t=t, b=b: e.dma_start(out=x2in[b][:], in_=x2S[t]), incs=[("x2ld", 16)])
                wait("vector", "x2ld")
                layernorm(x2in[b])
                transposes_xn()
                for iw, (wname, dst, bias) in enumerate((("qw", qTs, qbr), ("kw", None, kbr))):
                    for h in range(NH):
                        pb2 = (h % 2) * 128
                        wait("tensor", "s")
                        for c in range(8):
                            emit("tensor", lambda e, h=h, c=c, pb2=pb2, wname=wname: e.matmul(
                                pq64[0:64, pb2:pb2 + 128],
                                wb[wname][:, c * 1024 + h * 64:c * 1024 + h * 64 + 64],
                                xnT[:, c * 128:(c + 1) * 128],
                                start=(c == 0), stop=(c == 7)), incs=([("t", 1)] if c == 7 else []))
                        wait("scalar", "t")
                        if iw == 0:
                            emit("scalar", lambda e, h=h, pb2=pb2: e.activation(
                                out=qTs[0:64, h * 128:(h + 1) * 128], in_=pq64[0:64, pb2:pb2 + 128],
                                func=AF.Identity, bias=qbr[0:64, h:h + 1]), incs=[("s", 1)])
                        else:
                            emit("scalar", lambda e, h=h, pb2=pb2, slot=slot: e.activation(
                                out=kring[0:64, slot * 2048 + h * 128:slot * 2048 + (h + 1) * 128],
                                in_=pq64[0:64, pb2:pb2 + 128],
                                func=AF.Identity, bias=kbr[0:64, h:h + 1]), incs=[("s", 1)])
                wait("tensor", "s")
                for hf in range(2):
                    for c in range(8):
                        emit("tensor", lambda e, c=c, hf=hf: e.matmul(
                            pbig[:, hf * 512:hf * 512 + 512], xnT[:, c * 128:(c + 1) * 128],
                            wb["vw"][:, c * 1024 + hf * 512:c * 1024 + hf * 512 + 512],
                            start=(c == 0), stop=False))
                    emit("tensor", lambda e, hf=hf: e.matmul(
                        pbig[:, hf * 512:hf * 512 + 512], onesbf[:], vbbf[:, hf * 512:hf * 512 + 512],
                        start=False, stop=True), incs=([("t", 1)] if hf == 1 else []))
                wait("vector", "t")
                emit("vector", lambda e, slot=slot: e.tensor_copy(
                    out=vring[:, slot * 1024:(slot + 1) * 1024], in_=pbig[:, 0:1024]),
                    incs=[("v", 1)])
                slots = list(range(max(0, t - 2), t + 1))
                for si, k in enumerate(slots):
                    ks = k % 3
                    wait("tensor", "v")
                    wait("tensor", "s")
                    if si == 0:
                        wait("tensor", "v", swa_fin[0])
                        wait("tensor", "s", swa_sc[0])
                    for h in range(NH):
                        emit("tensor", lambda e, h=h, ks=ks: e.matmul(
                            pbig[:, h * 128:(h + 1) * 128],
                            kring[0:64, ks * 2048 + h * 128:ks * 2048 + (h + 1) * 128],
                            qTs[0:64, h * 128:(h + 1) * 128],
                            start=(h % 4 == 0), stop=(h % 4 == 3)), incs=([("t", 1)] if h == NH - 1 else []))
                    pt2 = ptb[si % 2]
                    wait("scalar", "t")
                    wait("scalar", "v", swa_fin[0])
                    wait("scalar", "t", pvmark.get(("spt", si % 2), 0))
                    emit("scalar", lambda e, pt2=pt2: e.activation(out=pt2[:], in_=pbig[:, 0:2048],
                                                                   func=AF.Exp, scale=SCL_SWA),
                         incs=[("s", 1)])
                    swa_sc[0] = cnt["s"]
                    msk = mlow if k == t else (mhigh if k == t - 2 else None)
                    if msk is not None:
                        wait("vector", "s")
                        for h in range(NH):
                            emit("vector", lambda e, h=h, pt2=pt2, msk=msk: e.tensor_tensor(
                                out=pt2[:, h * 128:(h + 1) * 128], in0=pt2[:, h * 128:(h + 1) * 128],
                                in1=msk[:], op=ALU.mult), incs=([("v", 1)] if h == NH - 1 else []))
                        wait("tensor", "v")
                    else:
                        wait("tensor", "s")
                    for h in range(NH):
                        emit("tensor", lambda e, h=h, pt2=pt2, ks=ks, si=si, n=len(slots): e.matmul(
                            acc[:, h * 64:(h + 1) * 64], pt2[:, h * 128:(h + 1) * 128],
                            vring[:, (ks * NH + h) * 64:(ks * NH + h) * 64 + 64],
                            start=(si == 0 and h % 8 == 0), stop=(si == n - 1 and h % 8 == 7)))
                        emit("tensor", lambda e, h=h, pt2=pt2, si=si, n=len(slots): e.matmul(
                            pden[:, h:h + 1], pt2[:, h * 128:(h + 1) * 128],
                            onecbf[:],
                            start=(si == 0 and h == 0), stop=(si == n - 1 and h == NH - 1)),
                            incs=([("t", 1)] if h == NH - 1 else []))
                    pvmark[("spt", si % 2)] = cnt["t"]
                ring_rd[slot] = cnt["t"]
                wait("vector", "t")
                emit("vector", lambda e: e.reciprocal(out=rec_sb[:, 0:NH], in_=pden[:, 0:NH]),
                     incs=[("v", 1)])
                wait("vector", "v")
                for h in range(NH):
                    emit("vector", lambda e, h=h: e.tensor_scalar(
                        out=attn_sb[:, h * 64:(h + 1) * 64], in0=acc[:, h * 64:h * 64 + 64],
                        scalar1=rec_sb[:, h:h + 1], scalar2=None, op0=ALU.mult),
                        incs=([("v", 1)] if h == NH - 1 else []))
                swa_fin[0] = cnt["v"]
                wait("tensor", "v")
                for h in range(NH):
                    pb2 = (h % 2) * 128
                    wait("tensor", "s")
                    emit("tensor", lambda e, h=h, pb2=pb2: e.transpose(
                        ptrb[0:64, pb2:pb2 + 128], attn_sb[:, h * 64:(h + 1) * 64], identbf[:]),
                        incs=[("t", 1)])
                    wait("scalar", "t")
                    emit("scalar", lambda e, h=h, pb2=pb2: e.copy(
                        out=attnT[(h % 2) * 64:(h % 2) * 64 + 64, (h // 2) * 128:(h // 2) * 128 + 128],
                        in_=ptrb[0:64, pb2:pb2 + 128]), incs=[("s", 1)])
                wait("tensor", "s")
                for hf in range(2):
                    for c in range(8):
                        emit("tensor", lambda e, c=c, hf=hf: e.matmul(
                            pbig[:, hf * 512:hf * 512 + 512], attnT[:, c * 128:(c + 1) * 128],
                            wb["ow"][:, c * 1024 + hf * 512:c * 1024 + hf * 512 + 512],
                            start=(c == 0), stop=False))
                    emit("tensor", lambda e, hf=hf: e.matmul(
                        pbig[:, hf * 512:hf * 512 + 512], onesbf[:], obbf[:, hf * 512:hf * 512 + 512],
                        start=False, stop=True), incs=([("t", 1)] if hf == 1 else []))
                wait("vector", "t")
                emit("vector", lambda e, b=b: e.tensor_tensor(out=x3sb[:], in0=pbig[:, 0:1024],
                                                              in1=x2in[b][:], op=ALU.add), incs=[("v", 1)])
                if t >= 1:
                    wait("vector", "ost")
                ffn(x3sb, x4sb[b])
                wait("gpsimd", "v")
                emit("gpsimd", lambda e, t=t, b=b: e.dma_start(
                    out=out.rearrange("(n p) d -> n p d", p=128)[t], in_=x4sb[b][:]), incs=[("ost", 16)])
            wait("gpsimd", "ost", nt * 16)

        block = E(nc.Block())

        @block.sync
        def _(e):
            program("sync")

        @block.scalar
        def _(e):
            program("scalar")

        @block.tensor
        def _(e):
            program("tensor")

        @block.vector
        def _(e):
            program("vector")

        @block.gpsimd
        def _(e):
            program("gpsimd")

    return nc


def _lay(w):
    k, m = w.shape
    return np.ascontiguousarray(w.reshape(k // 128, 128, m).transpose(1, 0, 2).reshape(128, -1))


def host_inputs(inp, S):
    perm32 = np.concatenate([np.arange(0, 32, 2), np.arange(1, 32, 2)])
    wqr = np.asarray(inp["L0_attn_a_W_QR"], np.float32).copy()
    wqr = wqr.reshape(DL, NH, 32)[:, :, perm32].reshape(DL, NH * 32)
    wkr = np.asarray(inp["L0_attn_a_W_KR"], np.float32)[:, perm32]
    d = {
        "wdq": _lay(np.asarray(inp["L0_attn_a_W_DQ"], np.float32)),
        "wuq": _lay(np.asarray(inp["L0_attn_a_W_UQ"], np.float32)),
        "wqr": _lay(wqr),
        "wdkv": _lay(np.asarray(inp["L0_attn_a_W_DKV"], np.float32)),
        "wuk": _lay(np.asarray(inp["L0_attn_a_W_UK"], np.float32)),
        "wuv": _lay(np.asarray(inp["L0_attn_a_W_UV"], np.float32)),
        "wkr": _lay(wkr),
        "wo": _lay(np.asarray(inp["L0_attn_a_W_O"], np.float32)),
        "u1": _lay(np.asarray(inp["L0_ffn_U1"], np.float32)),
        "v1": _lay(np.asarray(inp["L0_ffn_V1"], np.float32)),
        "u2": _lay(np.asarray(inp["L0_ffn_U2"], np.float32)),
        "v2": _lay(np.asarray(inp["L0_ffn_V2"], np.float32)),
        "qw": _lay(np.ascontiguousarray(np.asarray(inp["L0_attn_b_q_w"], np.float32).T)),
        "kw": _lay(np.ascontiguousarray(np.asarray(inp["L0_attn_b_k_w"], np.float32).T)),
        "vw": _lay(np.ascontiguousarray(np.asarray(inp["L0_attn_b_v_w"], np.float32).T)),
        "ow": _lay(np.ascontiguousarray(np.asarray(inp["L0_attn_b_o_w"], np.float32).T)),
        "b1r": np.ascontiguousarray(np.asarray(inp["L0_ffn_b1"], np.float32).reshape(32, 128).T),
        "qbr": np.ascontiguousarray(np.asarray(inp["L0_attn_b_q_b"], np.float32).reshape(16, 64).T),
        "kbr": np.ascontiguousarray(np.asarray(inp["L0_attn_b_k_b"], np.float32).reshape(16, 64).T),
        "b2row": np.asarray(inp["L0_ffn_b2"], np.float32).reshape(1, D),
        "vbrow": np.asarray(inp["L0_attn_b_v_b"], np.float32).reshape(1, D),
        "obrow": np.asarray(inp["L0_attn_b_o_b"], np.float32).reshape(1, D),
    }
    pos = np.arange(S, dtype=np.float32)
    inv = (10000.0 ** (-np.arange(0, 32, 2, dtype=np.float32) / 32.0))
    ang = pos[:, None] * inv[None, :]
    d["ropec"] = np.ascontiguousarray(np.tile(np.cos(ang), (1, 32)).astype(np.float32))
    d["ropes"] = np.ascontiguousarray(np.tile(np.sin(ang), (1, 32)).astype(np.float32))
    ii = np.arange(128)
    d["mlow"] = (ii[:, None] <= ii[None, :]).astype(np.float32)
    d["mhigh"] = (ii[:, None] > ii[None, :]).astype(np.float32)
    d["identbf"] = np.eye(128, dtype=np.float32).astype(ml_dtypes.bfloat16)
    d["onescol"] = np.ones((1, 128), np.float32)
    d["ones128"] = np.ones((128, 1), np.float32)
    d["cst"] = np.tile(np.array(CVALS, np.float32), (128, 1))
    return d


def kernel(**inputs):
    S = 4096
    nt = S // 128
    shared = host_inputs(inputs, S)
    x = np.asarray(inputs["L0_attn_a_x"], np.float32)
    nc = build(nt)
    in_maps = []
    for i in range(8):
        m = dict(shared)
        m["x"] = np.ascontiguousarray(x[i])
        in_maps.append(m)
    res = run_bass_kernel_spmd(nc, in_maps, core_ids=list(range(8)))
    out = np.stack([res.results[i]["out"] for i in range(8)], axis=0)
    return out.astype(np.float32)
